# version 31
# baseline (speedup 1.0000x reference)
import numpy as np
import ml_dtypes
import concourse.bass as bass
import concourse.mybir as mybir
from concourse.bass_utils import run_bass_kernel_spmd

F32 = mybir.dt.float32
BF16 = mybir.dt.bfloat16
U8 = mybir.dt.uint8
AF = mybir.ActivationFunctionType
ALU = mybir.AluOpType
AX = mybir.AxisListType.X
BF = ml_dtypes.bfloat16

D = 1024
KC = 8
H = 16
DH = 64
DFF = 4096
NE = 16
DE = 512
EPS_LN = 1e-5
EPS_ATTN = 1e-6

KB = 1024


class Prog:

    def __init__(self, t=2048, ncores=8, dbg=False):
        self.dbg = dbg
        self.dbg_t = {}
        assert t % 512 == 0
        self.t = t
        self.n512 = t // 512
        self.n128 = t // 128
        self.ncores = ncores
        self.xw = min(1024, t)
        self.nxh = t // self.xw
        nc = bass.Bass("TRN2", target_bir_lowering=False, debug=False,
                       detect_race_conditions=False)
        self.nc = nc
        self.eng = {"t": nc.tensor, "v": nc.vector, "s": nc.scalar,
                    "g": nc.gpsimd, "y": nc.sync}
        self.sems = {}
        for nm in ["sW0", "sW1", "sW2", "sW3", "sX0", "sX1", "sT", "sV", "sS",
                   "sCC", "sOut", "sOut1", "sOut2", "sG"]:
            h = nc.alloc_semaphore(name=nm)
            self.sems[nm] = [h, 0]
        self.last_wait = {}
        self.sems["sDbg"] = [nc.alloc_semaphore(name="sDbg"), 0]
        self._wrr = 0
        self._wgrp = {}
        self.declare_io()
        self.alloc_mem()

    def inc(self, ins, sem, k=1):
        rec = self.sems[sem]
        ins.then_inc(rec[0], k)
        rec[1] += k
        return (sem, rec[1])

    def tok(self, sem):
        return (sem, self.sems[sem][1])

    def wait(self, ename, *tokens):
        eng = self.eng[ename]
        for tk in tokens:
            if tk is None:
                continue
            sem, val = tk
            if val <= 0:
                continue
            key = (ename, sem)
            if self.last_wait.get(key, 0) < val:
                eng.wait_ge(self.sems[sem][0], val)
                self.last_wait[key] = val

    def declare_io(self):
        nc = self.nc
        t = self.t

        def din(name, shape, dt=F32):
            return nc.dram_tensor(name, shape, dt, kind="ExternalInput").ap()

        self.i_xT = din("xT", [D, t])
        self.i_Wq = din("Wq", [D, D]); self.i_Wk = din("Wk", [D, D])
        self.i_Wv = din("Wv", [D, D]); self.i_Wo = din("Wo", [D, D])
        self.i_W1 = din("W1", [D, DFF]); self.i_W2 = din("W2", [DFF, D])
        self.i_bqc = din("bq_col", [128, KC]); self.i_boc = din("bo_col", [128, KC])
        self.i_b1c = din("b1_col", [128, 32]); self.i_b2c = din("b2_col", [128, KC])
        self.i_bkr = din("bk_row", [1, D]); self.i_bvr = din("bv_row", [1, D])
        self.i_g1c = din("g1_col", [128, KC]); self.i_be1c = din("be1_col", [128, KC])
        self.i_g2c = din("g2_col", [128, KC]); self.i_be2c = din("be2_col", [128, KC])
        kpc = D // (128 * self.ncores)
        assert kpc >= 1 and D % (128 * self.ncores) == 0
        self.kpc = kpc
        self.i_ccol = din("c_col", [128, kpc])
        self.i_wmod = din("wmods", [D // self.ncores, 6 * D])
        self.i_rwT = din("rwT", [128, KC * NE]); self.i_rbr = din("rb_row", [1, NE])
        self.i_up = din("up_p", [NE, D, DE]); self.i_gp = din("gate_p", [NE, D, DE])
        self.i_dp = din("down_p", [NE, DE, D])
        self.i_shg = din("shgT", [D, D]); self.i_shu = din("shuT", [D, D])
        self.i_shd = din("shdT", [D, D])
        self.i_onesf = din("onesf", [1, 128])
        self.i_ones128b = din("ones128b", [128, 1], BF16)
        self.i_ones128f = din("ones128f", [128, 1])
        self.i_esel = din("esel", [H, D], BF16)
        self.i_selbank = din("selbank", [NE, NE * 128], BF16)
        self.i_id128 = din("id128", [128, 128], BF16)
        self.o_y = nc.dram_tensor("y", [D, t], F32, kind="ExternalOutput").ap()
        self.cc_in = nc.dram_tensor("cc_in", [128, 520], F32)
        self.cc_out = nc.dram_tensor("cc_out", [128, 520], F32)
        self.cc_min = nc.dram_tensor("cc_min", [128, 48], F32)
        self.cc_mout = nc.dram_tensor("cc_mout", [128, 48], F32)

    def alloc_mem(self):
        nc = self.nc
        t = self.t
        tb2 = t * 2
        tb4 = t * 4
        ARENA = 206 * KB
        self.arena = nc.alloc_sbuf_tensor("arena", [128, ARENA], U8)
        base = nc.lookup_mloc(self.arena).addr

        def at(name, shape, dt, off):
            per = int(np.prod(shape[1:])) * mybir.dt.size(dt)
            assert off % 32 == 0, (name, off)
            assert off + per <= ARENA, (name, off, per)
            return nc.alloc_sbuf_tensor_at(name, list(shape), dt, offset=base + off)

        o = 0
        self.onesf = at("onesf_sb", [1, 128], F32, o); o += 512
        self.ones128b = at("ones128b_sb", [128, 1], BF16, o); o += 32
        self.ones128f = at("ones128f_sb", [128, 1], F32, o); o += 32
        self.id128 = at("id128_sb", [128, 128], BF16, o); o += 256
        self.bqc = at("bqc", [128, KC], F32, o); o += 32
        self.boc = at("boc", [128, KC], F32, o); o += 32
        self.b1c = at("b1c", [128, 32], F32, o); o += 128
        self.b2c = at("b2c", [128, KC], F32, o); o += 32
        self.g1c = at("g1c", [128, KC], F32, o); o += 32
        self.be1c = at("be1c", [128, KC], F32, o); o += 32
        self.g2c = at("g2c", [128, KC], F32, o); o += 32
        self.be2c = at("be2c", [128, KC], F32, o); o += 32
        self.ccol = at("ccol", [128, KC], F32, o); o += 32
        self.ccolS = at("ccolS", [128, KC], F32, o); o += 32
        self.modcol = at("modcol", [128, 48], F32, o); o += 192
        self.acol = at("acol", [128, KC], F32, o); o += 32
        self.sacol = at("sacol", [128, KC], F32, o); o += 32
        self.gacol = at("gacol", [128, KC], F32, o); o += 32
        self.bcol = at("bcol", [128, KC], F32, o); o += 32
        self.sbcol = at("sbcol", [128, KC], F32, o); o += 32
        self.gbcol = at("gbcol", [128, KC], F32, o); o += 32
        self.g2ga = at("g2ga", [128, KC], F32, o); o += 32
        self.be2ga = at("be2ga", [128, KC], F32, o); o += 32
        self.esel = at("esel_sb", [H, D], BF16, o); o += 2 * KB
        self.selbank = at("selbank_sb", [NE, NE * 128], BF16, o); o += 4 * KB
        self.rwT = at("rwT_sb", [128, KC * NE], BF16, o); o += 256
        self.rbrep = at("rbrep", [128, NE], F32, o); o += 64
        self.combSB = at("combSB", [128, 512], F32, o); o += 2 * KB
        self.zcol = at("zcol", [128, 1], F32, o); o += 32
        o = (o + 31) // 32 * 32
        oA = o
        self.bkrep = at("bkrep", [128, D], F32, oA)
        self.bvrep = at("bvrep", [128, D], F32, oA + 4 * KB)
        self.kvacc = at("kvacc", [128, 8 * 65], F32, oA + 8 * KB)
        self.kvg = at("kvg", [128, 8 * 65], F32, oA + 10 * KB + 96)
        self.kvg16 = at("kvg16", [128, 8 * 65], BF16, oA + 12 * KB + 192)
        self.ksdf = at("ksdf", [128, 128], F32, oA + 13 * KB + 256)
        self.ksd16 = at("ksd16", [128, 128], BF16, oA + 13 * KB + 768)
        self.wo_a = at("wo_a", [128, KC * 512], BF16, oA)
        self.hid16r = [at("hid16a", [128, 4 * 512], BF16, oA),
                       at("hid16b", [128, 4 * 512], BF16, oA + 4 * KB)]
        self.shh16 = at("shh16", [128, 8 * 512], BF16, oA + 4 * KB)
        self.comb_all = at("comb_all", [128, 2048], BF16, oA + 8 * KB)
        o = oA + 14 * KB

        oP2 = o
        self.den = at("den", [16, 512], F32, oP2)
        self.zrow = at("zrow", [16, 512], F32, oP2 + 2 * KB)
        self.z16r = [at("z16a", [16, 512], BF16, oP2 + 4 * KB),
                     at("z16b", [16, 512], BF16, oP2 + 5 * KB)]
        self.zrep = at("zrep", [128, 512], F32, oP2 + 6 * KB)
        nt = self.n128
        self.combine_fm = at("combine_fm", [NE, t], BF16, oP2)
        oF = oP2 + 4 * KB
        self.ltm = at("ltm", [128, NE * nt], F32, oF)
        oF2 = oF + 1 * KB
        nt4 = (nt * 4 + 31) // 32 * 32
        self.mx1 = at("mx1", [128, nt], F32, oF2)
        self.mx2 = at("mx2", [128, nt], F32, oF2 + nt4)
        self.s1a = at("s1a", [128, nt], F32, oF2 + 2 * nt4)
        self.s2a = at("s2a", [128, nt], F32, oF2 + 3 * nt4)
        oF3 = oF2 + 4 * nt4
        self.msk1 = at("msk1", [128, NE * nt], F32, oF3)
        self.msk2 = at("msk2", [128, NE * nt], F32, oF3 + NE * nt * 4)
        self.comb16 = at("comb16", [128, NE * nt], BF16, oF3 + 2 * NE * nt * 4)
        assert oF3 + 2 * NE * nt * 4 + NE * nt * 2 <= oP2 + 8 * KB
        self.g16 = at("g16", [128, 8 * 512], BF16, oP2 + 8 * KB)
        o = oP2 + max(2 * tb4, 16 * KB)

        oP3 = o
        self.urep = at("urep", [128, t], F32, oP3)
        self.vrep = at("vrep", [128, t], F32, oP3 + tb4)
        self.rowA = at("rowA", [1, t], F32, oP3)
        self.rowB = at("rowB", [1, t], F32, oP3 + tb4)
        self.shdT = at("shdT_sb", [128, KC * KB], BF16, oP3)
        o = oP3 + max(2 * tb4, 16 * KB)

        self.b16 = at("b16", [128, KC * t], BF16, o); o += KC * tb2

        oBA = o
        self.bufa = at("bufa", [128, KC * t], F32, oBA)
        self.wmod_sb = at("wmod_sb", [128, KC * 512], F32, oBA)
        self.wk = at("wk_sb", [128, KC * KB], BF16, oBA + 16 * KB)
        self.wv = at("wv_sb", [128, KC * KB], BF16, oBA + 32 * KB)
        self.ketmp = at("ketmp", [128, D], F32, oBA + 48 * KB)
        self.k16r = [at("k16a", [128, D], BF16, oBA + 52 * KB),
                     at("k16b", [128, D], BF16, oBA + 54 * KB)]
        self.vsbr = [at("vsba", [128, H * 65], BF16, oBA + 56 * KB),
                     at("vsbb", [128, H * 65], BF16, oBA + 58 * KB + 512)]
        self.tmpA = at("tmpA", [128, 512], F32, oBA)
        self.tmpB = at("tmpB", [128, 512], F32, oBA + 2 * KB)
        self.tmpC = at("tmpC", [128, 512], F32, oBA + 4 * KB)
        self.tmpD = at("tmpD", [128, 512], F32, oBA + 6 * KB)
        self.xtmp2 = [at("xtmp2_0", [128, 1024], F32, oBA),
                      at("xtmp2_1", [128, 1024], F32, oBA + 4 * KB)]
        self.tmpE = at("tmpE", [128, 512], F32, oBA + 8 * KB)
        o = oBA + max(KC * tb4, 64 * KB)

        self.lntA = at("lntA", [128, 512], F32, o)
        self.comb_allB = at("comb_allB", [128, 2048], BF16, o)
        o += 2 * KB
        self.lntB = at("lntB", [128, 512], F32, o); o += 2 * KB

        oR2 = o
        self.xsq16 = at("xsq16", [128, KC * t], BF16, oR2)
        self.qp = at("qp", [128, KC * t], BF16, oR2)
        self.ffnslot = [at(f"ffns{i}", [128, KC * 512], BF16, oR2 + i * 8 * KB)
                        for i in range(4)]
        self.xsqF = at("xsqF", [128, KC * 512], BF16, oR2)
        self.exslot = [at(f"exs{i}", [128, KC * 512], BF16, oR2 + i * 8 * KB)
                       for i in range(4)]
        o = oR2 + max(KC * tb2, 32 * KB)

        oR4 = o
        self.xtmp = [at(f"xtmp{i}", [128, 1024], F32, oR4 + i * 4 * KB)
                     for i in range(2)]
        self.wq = at("wq_sb", [128, KC * KB], BF16, oR4)
        self.attn16 = at("attn16", [128, KC * 512], BF16, oR4)
        self.wo_b = at("wo_b", [128, KC * 512], BF16, oR4 + 8 * KB)
        self.xsqT = at("xsqT", [128, KC * 512], BF16, oR4)
        self.relu16 = at("relu16", [128, 8 * 512], BF16, oR4 + 8 * KB)
        self.exslot.append(at("exs4", [128, KC * 512], BF16, oR4))
        self.exslot.append(at("exs5", [128, KC * 512], BF16, oR4 + 8 * KB))
        o = oR4 + 16 * KB
        assert o <= ARENA, o

        self.ps = self.nc.alloc_psum_tensor("ps", [128, 7 * 512], F32)
        self.pstb = self.nc.alloc_psum_tensor("pstb", [128, 1024], BF16)
        self.bank = [self.ps[:, i * 512:(i + 1) * 512] for i in range(7)]
        self.bank_free = [None] * 8

    def bslice(self, b, lo, hi):
        return self.ps[:, b * 512 + lo: b * 512 + hi]

    def dump(self, name, sbuf_ap):
        if not self.dbg:
            return
        shape = list(sbuf_ap.shape)
        dt = sbuf_ap.dtype
        h = self.nc.dram_tensor("dbg_" + name, shape, dt, kind="ExternalOutput").ap()
        self.dbg_t[name] = (shape, dt)
        self.wait("y", self.tok("sV"), self.tok("sS"), self.tok("sT"),
                  self.tok("sG"))
        tk = self.dma("y", h, sbuf_ap, "sDbg")
        for e in ("t", "v", "s", "g"):
            self.wait(e, tk)

    def mm(self, out_ap, lhsT, rhs, start, stop, tile_position=None):
        return self.nc.tensor.matmul(out_ap, lhsT, rhs, start=start, stop=stop,
                                     tile_position=tile_position)

    def dma(self, q, out, in_, sem):
        self.wait(q, self.tok(sem))
        ins = self.eng[q].dma_start(out=out, in_=in_)
        return self.inc(ins, sem, 16)

    def wgrp_start(self):
        self._wgrp = {}

    def wdma(self, out, in_):
        sem = f"sW{self._wrr}"
        self._wrr = (self._wrr + 1) % 4
        tk = self.dma("g", out, in_, sem)
        self._wgrp[tk[0]] = tk[1]
        return tk

    def wgrp_end(self):
        return [ (k, v) for k, v in self._wgrp.items() ]

    def vfence(self, ins):
        tk = self.inc(ins, "sV")
        self.wait("v", tk)
        return tk

    def bwait(self, b):
        bf = self.bank_free[b]
        if bf is None:
            return
        if isinstance(bf, list):
            self.wait("t", *bf)
        else:
            self.wait("t", bf)

    def build(self):
        t = self.t
        nc = self.nc
        v, s, te, g, y = nc.vector, nc.scalar, nc.tensor, nc.gpsimd, nc.sync
        self.xtmp_free = [None, None]
        self._x_piece = {}
        self._x_cast = set()
        self._xtmp_vtok = ("sV", 0)

        for c_pre in (0, 1):
            slot = self.xtmp[c_pre % 2]
            h0 = self.dma("y", slot[0:64, 0:self.xw],
                          self.i_xT[c_pre * 128:c_pre * 128 + 64, 0:self.xw],
                          "sX0")
            h1 = self.dma("y", slot[64:128, 0:self.xw],
                          self.i_xT[c_pre * 128 + 64:(c_pre + 1) * 128, 0:self.xw],
                          "sX1")
            self._x_piece[(0, c_pre)] = (h0, h1)
        for dst, src in [
            (self.onesf, self.i_onesf), (self.ones128b, self.i_ones128b),
            (self.ones128f, self.i_ones128f),
            (self.id128, self.i_id128), (self.bqc, self.i_bqc),
            (self.boc, self.i_boc), (self.b1c, self.i_b1c), (self.b2c, self.i_b2c),
            (self.g1c, self.i_g1c), (self.be1c, self.i_be1c),
            (self.g2c, self.i_g2c), (self.be2c, self.i_be2c),
            (self.esel, self.i_esel),
            (self.selbank, self.i_selbank),
        ]:
            tk_consts = self.dma("y", dst[:, :], src, "sX0")
        tk_consts = self.dma("y", self.ccol[:, 0:self.kpc], self.i_ccol, "sX0")
        tk_rwt = self.wdma(self.rwT[:, :], self.i_rwT)

        wtk = {}
        for nm, dst, srcw in [("wk", self.wk, self.i_Wk), ("wv", self.wv, self.i_Wv)]:
            s3 = srcw.rearrange("(k p) j -> p k j", p=128)
            d3 = dst.ap().rearrange("p (k j) -> p k j", j=KB)
            wtk[nm] = [self.wdma(d3[:, 0:4, :], s3[:, 0:4, :]),
                       self.wdma(d3[:, 4:8, :], s3[:, 4:8, :])]

        v.memset(self.zcol[:, :], 0.0)
        self.wait("s", tk_consts)
        ins = s.activation(self.ccolS[:, 0:self.kpc], self.ccol[:, 0:self.kpc],
                           AF.Silu)
        tk_silu_c = self.inc(ins, "sS")
        psmod = self.bslice(6, 0, 48)
        kpc = self.kpc
        pw = 1024
        wm3 = self.wmod_sb.ap().rearrange("p (k j) -> p k j", j=pw)
        npc = 6 * D // pw
        for js in range(npc):
            tk_wm = self.wdma(wm3[:, 0:kpc, :],
                              self.i_wmod.rearrange("(k p) j -> p k j", p=128)[
                                  :, :, js * pw:(js + 1) * pw])
            self.wait("t", tk_silu_c, tk_wm)
            for jc in range(pw // 128):
                jg = js * (pw // 128) + jc
                for k in range(kpc):
                    ins = self.mm(psmod[:, jg:jg + 1],
                                  self.wmod_sb[:, k * pw + jc * 128: k * pw + jc * 128 + 128],
                                  self.ccolS[:, k:k + 1],
                                  start=(k == 0), stop=(k == kpc - 1))
            tk_mod_mm = self.inc(ins, "sT")
            if js < npc - 1:
                self.wait("g", tk_mod_mm)
        self.wait("v", tk_mod_mm, tk_consts)
        self.vfence(v.tensor_copy(self.modcol[:, :], psmod))
        self.wait("y", self.tok("sV"))
        tk_mcin = self.dma("y", self.cc_min.ap(), self.modcol[:, :], "sX0")
        self.wait("g", tk_mcin)
        ins = g.collective_compute(
            "AllReduce", ALU.add,
            replica_groups=[list(range(self.ncores))],
            ins=[self.cc_min.ap().opt()], outs=[self.cc_mout.ap().opt()])
        tk_mcc = self.inc(ins, "sCC")
        self.wait("g", tk_mcc)
        tk_mcb = self.dma("g", self.modcol[:, :], self.cc_mout.ap(), "sW3")

        def emit_mod_derive():
            self.wait("v", tk_mcb, tk_consts)
            mc = self.modcol
            v.tensor_copy(self.sacol[:, :], mc[:, 0:8])
            v.tensor_scalar(self.acol[:, :], mc[:, 8:16], 1.0, None, op0=ALU.add)
            v.tensor_copy(self.sbcol[:, :], mc[:, 24:32])
            v.tensor_scalar(self.bcol[:, :], mc[:, 32:40], 1.0, None, op0=ALU.add)
            for src_lo, dst in [(16, self.gacol), (40, self.gbcol)]:
                gsl = mc[:, src_lo:src_lo + 8]
                p1 = self.lntA[:, 0:8]; n1 = self.lntA[:, 8:16]
                mk = self.lntA[:, 16:24]
                v.tensor_scalar(p1, gsl, 0.25, None, op0=ALU.max)
                v.tensor_scalar(n1, gsl, -0.25, None, op0=ALU.min)
                self.vfence(v.tensor_scalar(mk, gsl, self.zcol[:, 0:1], None,
                                            op0=ALU.is_ge))
                v.tensor_tensor(p1, p1, mk, op=ALU.mult)
                self.vfence(v.tensor_scalar(mk, mk, -1.0, 1.0,
                                            op0=ALU.mult, op1=ALU.add))
                self.vfence(v.tensor_tensor(n1, n1, mk, op=ALU.mult))
                ins = v.tensor_tensor(dst[:, :], p1, n1, op=ALU.add)
            self.inc(ins, "sV")
            v.tensor_tensor(self.g2ga[:, :], self.g2c[:, :], self.gacol[:, :],
                            op=ALU.mult)
            v.tensor_tensor(self.be2ga[:, :], self.be2c[:, :], self.gacol[:, :],
                            op=ALU.mult)

        vs3r = [sb.ap().rearrange("p (h m) -> p h m", m=65) for sb in self.vsbr]
        for vs3 in vs3r:
            v.memset(vs3[:, :, 64:65], 1.0)

        def _lap(src_ap, c, tt0, sc_col, bi_col, out_fp32, out_bf16, gp=False):
            pp = (self.lntB, self.combSB)[c % 2]
            v.scalar_tensor_tensor(pp[:, :], src_ap, sc_col[:, c:c + 1],
                                   self.urep[:, tt0:tt0 + 512],
                                   op0=ALU.mult, op1=ALU.mult)
            ins = v.scalar_tensor_tensor(pp[:, :], self.vrep[:, tt0:tt0 + 512],
                                         sc_col[:, c:c + 1], pp[:, :],
                                         op0=ALU.mult, op1=ALU.add)
            self.inc(ins, "sV")
            if out_fp32 is not None:
                ins = v.tensor_scalar(out_fp32(c, tt0), pp[:, :],
                                      bi_col[:, c:c + 1], None, op0=ALU.add)
                tkf = self.inc(ins, "sV")
                if out_bf16 is not None:
                    self.wait("s", tkf)
                    ins = s.copy(out_bf16(c, tt0), out_fp32(c, tt0))
                    self.inc(ins, "sS")
            elif out_bf16 is not None:
                ins = v.tensor_scalar(out_bf16(c, tt0), pp[:, :],
                                      bi_col[:, c:c + 1], None, op0=ALU.add)
                tkf = self.inc(ins, "sV")

        def _ln_tail_tt(tt):
            sl = slice(tt * 512, (tt + 1) * 512)
            ra, rb = self.rowA[0:1, sl], self.rowB[0:1, sl]
            rt = self.lntA[0:1, 0:512]
            v.tensor_scalar(ra, ra, 1.0 / D, None, op0=ALU.mult)
            v.tensor_scalar(rb, rb, 1.0 / D, None, op0=ALU.mult)
            v.tensor_tensor(rt, ra, ra, op=ALU.mult)
            v.tensor_tensor(rb, rb, rt, op=ALU.subtract)
            ins = v.tensor_scalar(rb, rb, EPS_LN, None, op0=ALU.add)
            tk_var = self.inc(ins, "sV")
            self.wait("s", tk_var)
            ins = s.activation(rb, rb, AF.Sqrt)
            tk_sd = self.inc(ins, "sS")
            self.wait("t", tk_var, tk_sd)
            self.bwait(0); self.bwait(1)
            self.mm(self.bank[0], self.onesf[:, :], rb, True, True)
            iv = self.mm(self.bank[1], self.onesf[:, :], ra, True, True)
            tk_mm = self.inc(iv, "sT")
            self.wait("v", tk_mm)
            v.reciprocal(self.urep[:, sl], self.bank[0])
            ins = v.scalar_tensor_tensor(self.vrep[:, sl], self.bank[1], -1.0,
                                         self.urep[:, sl],
                                         op0=ALU.mult, op1=ALU.mult)
            tk_ev = self.inc(ins, "sV")
            self.bank_free[0] = tk_ev
            self.bank_free[1] = tk_ev
            self._uv_tok = tk_ev

        def ln_dram(src_dram, aff, out_bf16, pre_apply=None):
            tk_t0 = self.tok("sT")
            self.wait("v", tk_t0); self.wait("s", tk_t0)
            sc_col, bi_col = aff
            tk_apply = []
            xw = self.xw
            pend_ap = None

            def do_applies(ta):
                for c in range(KC):
                    _lap(self.b16[:, c * t + ta * 512: c * t + ta * 512 + 512],
                         c, ta * 512, sc_col, bi_col, None, out_bf16)
                tk_apply.append((self.tok("sV"), self.tok("sS")))

            xslots = self.xtmp + self.xtmp2
            xfree = {i: self.xtmp_free[i % 2] for i in range(4)}
            for tt in range(self.n512):
                hh = (tt * 512) // xw
                for c in range(KC):
                    si = c % 4
                    slot = xslots[si]
                    if self._x_piece.get((hh, c)) is None:
                        self.wait("y", xfree[si])
                        h0 = self.dma("y", slot[0:64, 0:xw],
                                      src_dram(c)[0:64, hh * xw:(hh + 1) * xw],
                                      "sX0")
                        h1 = self.dma("y", slot[64:128, 0:xw],
                                      src_dram(c)[64:128, hh * xw:(hh + 1) * xw],
                                      "sX1")
                        self._x_piece[(hh, c)] = (h0, h1)
                    if (hh, c) not in self._x_cast:
                        h0, h1 = self._x_piece[(hh, c)]
                        self.wait("v", h0, h1); self.wait("s", h0, h1)
                        col = c * t + hh * xw
                        insv = v.tensor_copy(self.b16[:, col:col + xw],
                                             slot[:, 0:xw])
                        inss = s.activation(self.xsq16[:, col:col + xw],
                                            slot[:, 0:xw], AF.Square)
                        self.inc(inss, "sS")
                        xfree[si] = self.inc(insv, "sV")
                        self._x_cast.add((hh, c))
                tkv, tks = self.tok("sV"), self.tok("sS")
                self.bwait(4); self.bwait(5)
                self.wait("t", tkv, tks)
                for c in range(KC):
                    sl2 = slice(c * t + tt * 512, c * t + tt * 512 + 512)
                    self.mm(self.bslice(4, 0, 512)[0:1, :], self.ones128b[:, :],
                            self.b16[:, sl2], start=(c == 0), stop=(c == KC - 1))
                    ins = self.mm(self.bslice(5, 0, 512)[0:1, :], self.ones128b[:, :],
                                  self.xsq16[:, sl2], start=(c == 0),
                                  stop=(c == KC - 1))
                tk_mm = self.inc(ins, "sT")
                self.wait("v", tk_mm)
                v.tensor_copy(self.rowA[0:1, tt * 512:(tt + 1) * 512],
                              self.bslice(4, 0, 512)[0:1, :])
                ins = v.tensor_copy(self.rowB[0:1, tt * 512:(tt + 1) * 512],
                                    self.bslice(5, 0, 512)[0:1, :])
                tk_ev = self.inc(ins, "sV")
                self.bank_free[4] = tk_ev
                self.bank_free[5] = tk_ev
                _ln_tail_tt(tt)
                if pend_ap is not None:
                    if pre_apply is not None:
                        pre_apply()
                        pre_apply = None
                    do_applies(pend_ap)
                pend_ap = tt
            self._xtmp23_free = (xfree[2], xfree[3])
            self.xtmp_free = [xfree[0], xfree[1]]
            if pre_apply is not None:
                pre_apply()
            do_applies(pend_ap)
            return tk_apply

        def ln_sbuf(src_sbuf, aff, out_fp32, out_bf16, src_ready=None,
                    xsq=None, gp=False):
            if xsq is None:
                xsq = self.xsqT
            if src_ready is None:
                tk_t0 = self.tok("sT")
                tk_v0 = self.tok("sV")
                self.wait("v", tk_t0)
                self.wait("s", tk_t0, tk_v0)
                self.wait("t", tk_v0)
                if gp:
                    self.wait("g", tk_t0, tk_v0)
            tk_apply = []
            pend_ap = None
            sc_col, bi_col = aff

            def do_applies(ta):
                for c in range(KC):
                    _lap(src_sbuf(c)[:, ta * 512:(ta + 1) * 512], c, ta * 512,
                         sc_col, bi_col, out_fp32, out_bf16, gp=gp)
                tk_apply.append((self.tok("sV"), self.tok("sS")))

            for tt in range(self.n512):
                if src_ready is not None:
                    self.wait("v", *src_ready[tt])
                    self.wait("s", *src_ready[tt])
                    if gp:
                        self.wait("g", *src_ready[tt])
                for c in range(KC):
                    sl = src_sbuf(c)[:, tt * 512:(tt + 1) * 512]
                    ins = s.activation(xsq[:, c * 512:(c + 1) * 512], sl, AF.Square)
                tk_sq = self.inc(ins, "sS")
                self.bwait(4); self.bwait(5)
                self.wait("t", tk_sq)
                if src_ready is not None:
                    self.wait("t", *src_ready[tt])
                for c in range(KC):
                    sl = src_sbuf(c)[:, tt * 512:(tt + 1) * 512]
                    self.mm(self.bslice(4, 0, 512)[0:1, :], self.ones128f[:, :],
                            sl, start=(c == 0), stop=(c == KC - 1))
                    ins = self.mm(self.bslice(5, 0, 512)[0:1, :], self.ones128b[:, :],
                                  xsq[:, c * 512:(c + 1) * 512],
                                  start=(c == 0), stop=(c == KC - 1))
                tk_mm = self.inc(ins, "sT")
                self.wait("s", tk_mm)
                self.wait("v", tk_mm)
                v.tensor_copy(self.rowA[0:1, tt * 512:(tt + 1) * 512],
                              self.bslice(4, 0, 512)[0:1, :])
                ins = v.tensor_copy(self.rowB[0:1, tt * 512:(tt + 1) * 512],
                                    self.bslice(5, 0, 512)[0:1, :])
                tk_ev = self.inc(ins, "sV")
                self.bank_free[4] = tk_ev
                self.bank_free[5] = tk_ev
                _ln_tail_tt(tt)
                if pend_ap is not None:
                    do_applies(pend_ap)
                pend_ap = tt
            do_applies(pend_ap)
            return tk_apply

        def xsrc(c):
            return self.i_xT[c * 128:(c + 1) * 128, :]

        tk_h_apply = ln_dram(xsrc, (self.acol, self.sacol),
                lambda c, tt0: self.b16[:, c * t + tt0: c * t + tt0 + 512],
                pre_apply=emit_mod_derive)

        self.wait("y", *self._xtmp23_free)
        for half in range(2):
            sl = slice(half * 512, (half + 1) * 512)
            ta = self.dma("y", self.tmpA[0:1, 0:512], self.i_bkr[0:1, sl], "sX0")
            tb = self.dma("y", self.tmpB[0:1, 0:512], self.i_bvr[0:1, sl], "sX1")
            self.wait("t", ta, tb, tk_consts)
            self.bwait(4); self.bwait(5)
            self.mm(self.bank[4], self.onesf[:, :], self.tmpA[0:1, 0:512], True, True)
            i1 = self.mm(self.bank[5], self.onesf[:, :], self.tmpB[0:1, 0:512],
                         True, True)
            tk_bc = self.inc(i1, "sT")
            self.wait("v", tk_bc)
            v.tensor_copy(self.bkrep[:, sl], self.bank[4])
            ins = v.tensor_copy(self.bvrep[:, sl], self.bank[5])
            tk_rep = self.inc(ins, "sV")
            self.bank_free[4] = tk_rep
            self.bank_free[5] = tk_rep
            self.wait("y", tk_rep)
        rowrb = self.dma("y", self.tmpA[0:1, 0:NE], self.i_rbr, "sX0")
        self.wait("t", rowrb)
        self.bwait(4)
        ins = self.mm(self.bank[4][:, 0:NE], self.onesf[:, :],
                      self.tmpA[0:1, 0:NE], True, True)
        tk_rb = self.inc(ins, "sT")
        self.wait("v", tk_rb)
        ins = v.tensor_copy(self.rbrep[:, :], self.bank[4][:, 0:NE])
        tk_rbv = self.inc(ins, "sV")
        self.bank_free[4] = tk_rbv
        tk_h16 = self.tok("sV")
        self.dump("modcol", self.modcol[:, :])
        self.dump("h16", self.b16[:, :])
        self.dump("urep0", self.urep[:, :])
        self.dump("vrep0", self.vrep[:, :])

        self.wait("g", self.xtmp_free[0], self.xtmp_free[1], self._xtmp_vtok)
        wtk["wq"] = [self.wdma(self.wq[:, :],
                               self.i_Wq.rearrange("(k p) j -> p k j", p=128))]
        ins = v.memset(self.kvacc[:, :], 0.0)
        self.inc(ins, "sV")
        self.wait("t", *wtk["wk"], *wtk["wv"])
        kva = self.kvacc.ap().rearrange("p (c m) -> p c m", m=65)
        kv_done = [None, None]
        tk_kvacc = None

        def emit_kv(pk):
            nonlocal tk_kvacc
            r_, tk_k16_, tk_vsb_ = pk
            self.wait("t", tk_k16_, tk_vsb_)
            self.bwait(0); self.bwait(1)
            for h in range(H):
                pair = h // 2
                b = pair // 4
                col0 = (pair % 4) * 128
                odd = h % 2
                outap = self.ps[odd * 64:(odd + 1) * 64,
                                b * 512 + col0: b * 512 + col0 + 65]
                ins = self.mm(outap, self.k16r[r_][:, h * 64:(h + 1) * 64],
                              self.vsbr[r_][:, h * 65:(h + 1) * 65], True, True,
                              tile_position=(0, odd * 64))
            tk_kv = self.inc(ins, "sT")
            kv_done[r_] = tk_kv
            self.wait("v", tk_kv)
            for bi in range(2):
                psa = self.ps[:, bi * 512: bi * 512 + 512].rearrange(
                    "p (c m) -> p c m", m=128)
                ins = v.tensor_tensor(kva[:, bi * 4:(bi + 1) * 4, :],
                                      kva[:, bi * 4:(bi + 1) * 4, :],
                                      psa[:, :, 0:65], op=ALU.add)
            tk_kvacc = self.inc(ins, "sV")
            self.bank_free[0] = tk_kvacc
            self.bank_free[1] = tk_kvacc

        pend_kv = None
        for i in range(self.n128):
            self.wait("t", *tk_h_apply[(i * 128) // 512])
            r = i % 2
            vs3 = vs3r[r]
            for jh in range(2):
                b = (2, 3, 6)[(i * 2 + jh) % 3]
                self.bwait(b)
                for k in range(KC):
                    ins = self.mm(self.bank[b],
                                  self.b16[:, k * t + i * 128: k * t + i * 128 + 128],
                                  self.wk[:, k * KB + jh * 512: k * KB + (jh + 1) * 512],
                                  start=(k == 0), stop=(k == KC - 1))
                tk_k = self.inc(ins, "sT")
                self.wait("v", tk_k, kv_done[r])
                v.tensor_tensor(self.ketmp[:, jh * 512:(jh + 1) * 512], self.bank[b],
                                self.bkrep[:, jh * 512:(jh + 1) * 512], op=ALU.add)
                ins = v.tensor_scalar(self.tmpA[:, :],
                                      self.ketmp[:, jh * 512:(jh + 1) * 512],
                                      0.0, None, op0=ALU.min)
                tk_t1 = self.inc(ins, "sV")
                self.bank_free[b] = tk_t1
                self.wait("s", tk_t1)
                ins = s.activation(self.tmpB[:, :], self.tmpA[:, :], AF.Exp)
                tk_t2 = self.inc(ins, "sS")
                self.wait("v", tk_t2)
                ins = v.scalar_tensor_tensor(self.k16r[r][:, jh * 512:(jh + 1) * 512],
                                             self.ketmp[:, jh * 512:(jh + 1) * 512],
                                             0.0, self.tmpB[:, :],
                                             op0=ALU.max, op1=ALU.add)
                tk_k16 = self.inc(ins, "sV")
            for jh in range(2):
                b = 4 + jh
                self.bwait(b)
                for k in range(KC):
                    ins = self.mm(self.bank[b],
                                  self.b16[:, k * t + i * 128: k * t + i * 128 + 128],
                                  self.wv[:, k * KB + jh * 512: k * KB + (jh + 1) * 512],
                                  start=(k == 0), stop=(k == KC - 1))
                tk_v = self.inc(ins, "sT")
                self.wait("v", tk_v)
                ins = v.tensor_tensor(vs3[:, jh * 8:(jh + 1) * 8, 0:64], self.bank[b],
                                      self.bvrep[:, jh * 512:(jh + 1) * 512],
                                      op=ALU.add)
                tk_vsb = self.inc(ins, "sV")
                self.bank_free[b] = tk_vsb
            if pend_kv is not None:
                emit_kv(pend_kv)
            pend_kv = (r, tk_k16, tk_vsb)
        emit_kv(pend_kv)

        self.dump("kvacc", self.kvacc[:, :])
        self.wait("y", tk_kvacc)
        tk_ccin = self.dma("y", self.cc_in.ap(), self.kvacc[:, :], "sX0")
        self.wait("g", tk_ccin)
        ins = g.collective_compute(
            "AllReduce", ALU.add,
            replica_groups=[list(range(self.ncores))],
            ins=[self.cc_in.ap().opt()], outs=[self.cc_out.ap().opt()])
        tk_cc = self.inc(ins, "sCC")

        self.wait("g", self.tok("sV"))
        tk_wo_a = [self.wdma(self.wo_a[:, :],
                             self.i_Wo.rearrange("(k p) j -> p k j", p=128)[:, :, 0:512])]
        self.wait("t", *wtk["wq"])
        qp_hist = {0: None, 1: None}
        for jc in range(KC):
            for tt in range(self.n512):
                u_i = jc * self.n512 + tt
                pp = u_i % 2
                rl_buf = (self.tmpC, self.tmpD)[pp]
                ex_buf = (self.tmpB, self.tmpE)[pp]
                b = (2, 3, 4, 5, 6)[u_i % 5]
                self.bwait(b)
                for k in range(KC):
                    ins = self.mm(self.bank[b],
                                  self.wq[:, k * KB + jc * 128: k * KB + jc * 128 + 128],
                                  self.b16[:, k * t + tt * 512: k * t + (tt + 1) * 512],
                                  start=(k == 0), stop=(k == KC - 1))
                tk_q = self.inc(ins, "sT")
                self.wait("v", tk_q)
                ins = v.tensor_scalar(self.tmpA[:, :], self.bank[b],
                                      self.bqc[:, jc:jc + 1], None, op0=ALU.add)
                tk_qc = self.inc(ins, "sV")
                self.bank_free[b] = tk_qc
                self.wait("s", tk_qc, qp_hist[pp])
                ins = s.activation(rl_buf[:, :], self.tmpA[:, :], AF.Relu)
                tk_rl = self.inc(ins, "sS")
                self.wait("v", tk_rl)
                ins = v.tensor_tensor(self.tmpA[:, :], self.tmpA[:, :],
                                      rl_buf[:, :], op=ALU.subtract)
                tk_mn = self.inc(ins, "sV")
                self.wait("s", tk_mn)
                ins = s.activation(ex_buf[:, :], self.tmpA[:, :], AF.Exp)
                tk_qe = self.inc(ins, "sS")
                self.wait("v", tk_qe)
                ins = v.tensor_tensor(
                    self.qp[:, jc * t + tt * 512: jc * t + (tt + 1) * 512],
                    rl_buf[:, :], ex_buf[:, :], op=ALU.add)
                tk_qp = self.inc(ins, "sV")
                qp_hist[pp] = tk_qp

        self.wait("g", tk_cc)
        tk_kvg = self.wdma(self.kvg[:, :], self.cc_out.ap())
        self.wait("v", tk_kvg)
        ins = v.tensor_copy(self.kvg16[:, :], self.kvg[:, :])
        self.inc(ins, "sV")
        v.memset(self.ksdf[:, :], 0.0)
        for h in range(H):
            k = h // 2
            odd = h % 2
            ins = v.tensor_copy(
                self.ksdf[odd * 64:(odd + 1) * 64, k * 16 + h:k * 16 + h + 1],
                self.kvg[odd * 64:(odd + 1) * 64, k * 65 + 64: k * 65 + 65])
        self.vfence(ins)
        ins = v.tensor_copy(self.ksd16[:, :], self.ksdf[:, :])
        tk_ksd = self.inc(ins, "sV")

        self.dump("qp", self.qp[:, :])
        self.dump("kvg", self.kvg[:, :])
        self.dump("ksdf", self.ksdf[:, :])
        self.wait("g", self.tok("sT"), self.tok("sV"))
        tk_wo_grp = [self.wdma(
            self.wo_b[:, :],
            self.i_Wo.rearrange("(k p) j -> p k j", p=128)[:, :, 512:1024])] + tk_wo_a

        def wo_lhs(k, jc):
            if jc < 4:
                return self.wo_a[:, k * 512 + jc * 128: k * 512 + jc * 128 + 128]
            return self.wo_b[:, k * 512 + (jc - 4) * 128: k * 512 + (jc - 4) * 128 + 128]

        self.wait("t", tk_ksd, tk_qp)
        tk_c1_tt = []
        tk_z_r = [None, None]

        def emit_den(tt):
            self.bwait(4)
            for k in range(KC):
                ins = self.mm(self.bslice(4, 0, 512)[0:16, :],
                              self.ksd16[:, k * 16:(k + 1) * 16],
                              self.qp[:, k * t + tt * 512: k * t + (tt + 1) * 512],
                              start=(k == 0), stop=(k == KC - 1))
            tk_den = self.inc(ins, "sT")
            self.wait("v", tk_den)
            if tk_z_r[tt % 2] is not None:
                self.wait("v", self._z_read[tt % 2])
            ins = v.tensor_scalar(self.den[:, :], self.bslice(4, 0, 512)[0:16, :],
                                  EPS_ATTN, None, op0=ALU.add)
            tk_dv = self.inc(ins, "sV")
            self.bank_free[4] = tk_dv
            v.reciprocal(self.zrow[:, :], self.den[:, :])
            ins = v.tensor_copy(self.z16r[tt % 2][:, :], self.zrow[:, :])
            tk_z_r[tt % 2] = self.inc(ins, "sV")

        self._z_read = [None, None]
        emit_den(0)
        for tt in range(self.n512):
            if tt + 1 < self.n512:
                emit_den(tt + 1)
            z16 = self.z16r[tt % 2]
            tk_z = tk_z_r[tt % 2]
            self.wait("t", tk_z)
            for c in range(KC):
                bz = 5 if c % 2 == 0 else 2
                self.bwait(bz)
                ins = self.mm(self.bank[bz], self.esel[:, c * 128:(c + 1) * 128],
                              z16[:, :], True, True)
                tk_zr = self.inc(ins, "sT")
                self.wait("v", tk_zr)
                ins = v.tensor_copy(self.zrep[:, (c % 2) * 0:512], self.bank[bz])
                tk_zrv = self.inc(ins, "sV")
                self.bank_free[bz] = tk_zrv
                self.bwait(6)
                self.mm(self.ps[0:64, 6 * 512:7 * 512],
                        self.kvg16[0:64, c * 65: c * 65 + 64],
                        self.qp[0:64, c * t + tt * 512: c * t + (tt + 1) * 512],
                        True, True)
                i1 = self.mm(self.ps[64:128, 6 * 512:7 * 512],
                             self.kvg16[64:128, c * 65: c * 65 + 64],
                             self.qp[64:128, c * t + tt * 512: c * t + (tt + 1) * 512],
                             True, True, tile_position=(64, 64))
                tk_num = self.inc(i1, "sT")
                self.wait("v", tk_num, tk_zrv)
                ins = v.tensor_tensor(self.attn16[:, c * 512:(c + 1) * 512],
                                      self.bank[6], self.zrep[:, :], op=ALU.mult)
                tk_at = self.inc(ins, "sV")
                self.bank_free[6] = tk_at
            self._z_read[tt % 2] = self.tok("sT")
            self.wait("t", tk_at, *tk_wo_grp)
            for jc in range(KC):
                b = (2, 3, 0, 1)[jc % 4]
                self.bwait(b)
                for k in range(KC):
                    self.mm(self.bank[b], wo_lhs(k, jc),
                            self.attn16[:, k * 512:(k + 1) * 512],
                            start=(k == 0), stop=False)
                ins = self.mm(self.bank[b], self.id128[:, :],
                              self.b16[:, jc * t + tt * 512: jc * t + (tt + 1) * 512],
                              start=False, stop=True)
                tk_womm = self.inc(ins, "sT")
                self.wait("v", tk_womm)
                ins = v.tensor_scalar(
                    self.bufa[:, jc * t + tt * 512: jc * t + (tt + 1) * 512],
                    self.bank[b], self.boc[:, jc:jc + 1], None, op0=ALU.add)
                tk_bo = self.inc(ins, "sV")
                self.bank_free[b] = tk_bo
            tk_c1_tt.append((tk_bo, tk_bo))

        self.dump("proj", self.bufa[:, :])
        self.dump("attn16_last", self.attn16[:, :])
        self.dump("zrow_last", self.zrow[:, :])
        ffn_tok = {}

        def load_ffn_eighth(qq, slot_w1, slot_w2):
            self.wgrp_start()
            self.wdma(self.ffnslot[slot_w1][:, :],
                      self.i_W1.rearrange("(k p) j -> p k j", p=128)[
                          :, :, qq * 512:(qq + 1) * 512])
            self.wdma(self.ffnslot[slot_w2][:, :],
                      self.i_W2[qq * 512:(qq + 1) * 512, :].rearrange(
                          "(k p) j -> p k j", p=128))
            ffn_tok[qq] = self.wgrp_end()

        self.wait("g", self.tok("sT"))
        load_ffn_eighth(0, 0, 1)
        load_ffn_eighth(1, 2, 3)

        def bufa_c(c):
            return self.bufa[:, c * t:(c + 1) * t]

        tk_apply1 = ln_sbuf(bufa_c, (self.g1c, self.be1c),
                lambda c, tt0: self.bufa[:, c * t + tt0: c * t + tt0 + 512],
                lambda c, tt0: self.b16[:, c * t + tt0: c * t + tt0 + 512],
                src_ready=tk_c1_tt, xsq=self.g16)

        self.dump("a_ln", self.bufa[:, :])
        self.dump("a_ln16", self.b16[:, :])
        self.xtmp_free = [self.tok("sT"), self.tok("sT")]
        self._xtmp_vtok = self.tok("sT")

        nq = 8
        tk_ffn_tt = []
        for qq in range(nq):
            s_w1 = (qq % 2) * 2
            s_w2 = s_w1 + 1
            self.wait("t", *ffn_tok[qq])
            for tt in range(self.n512):
                self.wait("t", *tk_apply1[tt])
                for jj in range(4):
                    b = 2 + jj % 2
                    self.bwait(b)
                    for k in range(KC):
                        ins = self.mm(
                            self.bank[b],
                            self.ffnslot[s_w1][:, k * 512 + jj * 128: k * 512 + jj * 128 + 128],
                            self.b16[:, k * t + tt * 512: k * t + (tt + 1) * 512],
                            start=(k == 0), stop=(k == KC - 1))
                    tk_w1 = self.inc(ins, "sT")
                    self.wait("s", tk_w1)
                    ins = s.activation(self.relu16[:, jj * 512:(jj + 1) * 512],
                                       self.bank[b], AF.Relu,
                                       bias=self.b1c[:, qq * 4 + jj: qq * 4 + jj + 1])
                    tk_r = self.inc(ins, "sS")
                    self.bank_free[b] = tk_r
                self.wait("t", tk_r)
                for jc in range(KC):
                    b = (5, 6, 0, 1)[jc % 4]
                    self.bwait(b)
                    for kk in range(4):
                        ins = self.mm(
                            self.bank[b],
                            self.ffnslot[s_w2][:, kk * KB + jc * 128: kk * KB + jc * 128 + 128],
                            self.relu16[:, kk * 512:(kk + 1) * 512],
                            start=(kk == 0), stop=(kk == 3))
                    tk_w2 = self.inc(ins, "sT")
                    self.wait("v", tk_w2)
                    dst = self.bufa[:, jc * t + tt * 512: jc * t + (tt + 1) * 512]
                    if qq == 0:
                        ins = v.scalar_tensor_tensor(dst, self.bank[b],
                                                     self.b2c[:, jc:jc + 1], dst,
                                                     op0=ALU.add, op1=ALU.add)
                    else:
                        ins = v.tensor_tensor(dst, self.bank[b], dst, op=ALU.add)
                    tk_y2 = self.inc(ins, "sV")
                    self.bank_free[b] = tk_y2
                if qq == nq - 1:
                    tk_ffn_tt.append((self.tok("sV"), self.tok("sV")))
            if qq + 2 < nq:
                self.wait("g", self.tok("sT"))
                load_ffn_eighth(qq + 2, s_w1, s_w2)

        def load_mat_16k(dst2, srcw):
            self.wgrp_start()
            self.wdma(self.exslot[dst2[0]][:, :],
                      srcw[0:512, :].rearrange("(k p) j -> p k j", p=128))
            self.wdma(self.exslot[dst2[1]][:, :],
                      srcw[512:1024, :].rearrange("(k p) j -> p k j", p=128))
            return self.wgrp_end()

        self.dump("s_ffn", self.bufa[:, :])
        self.wait("g", self.tok("sT"), self.tok("sS"))
        tk_shg = load_mat_16k((0, 1), self.i_shg)
        tk_shu = load_mat_16k((2, 3), self.i_shu)
        xw = self.xw
        x_tok = {}

        def load_x_piece(pidx):
            hh, c = pidx // KC, pidx % KC
            slot = self.xtmp[pidx % 2]
            self.wait("y", self.xtmp_free[pidx % 2])
            h0 = self.dma("y", slot[0:64, 0:xw],
                          xsrc(c)[0:64, hh * xw:(hh + 1) * xw], "sX0")
            h1 = self.dma("y", slot[64:128, 0:xw],
                          xsrc(c)[64:128, hh * xw:(hh + 1) * xw], "sX1")
            x_tok[pidx] = (h0, h1)

        load_x_piece(0)
        load_x_piece(1)
        for tt in range(self.n512):
            self.wait("v", *tk_ffn_tt[tt])
            self.wait("s", *tk_ffn_tt[tt])
            for c in range(KC):
                sl = bufa_c(c)[:, tt * 512:(tt + 1) * 512]
                ins = s.activation(self.g16[:, c * 512:(c + 1) * 512], sl,
                                   AF.Square)
            tk_sq = self.inc(ins, "sS")
            self.bwait(4); self.bwait(5)
            self.wait("t", tk_sq, *tk_ffn_tt[tt])
            for c in range(KC):
                sl = bufa_c(c)[:, tt * 512:(tt + 1) * 512]
                self.mm(self.bslice(4, 0, 512)[0:1, :], self.ones128f[:, :],
                        sl, start=(c == 0), stop=(c == KC - 1))
                ins = self.mm(self.bslice(5, 0, 512)[0:1, :], self.ones128b[:, :],
                              self.g16[:, c * 512:(c + 1) * 512],
                              start=(c == 0), stop=(c == KC - 1))
            tk_mm = self.inc(ins, "sT")
            self.wait("s", tk_mm)
            self.wait("v", tk_mm)
            v.tensor_copy(self.rowA[0:1, tt * 512:(tt + 1) * 512],
                          self.bslice(4, 0, 512)[0:1, :])
            ins = v.tensor_copy(self.rowB[0:1, tt * 512:(tt + 1) * 512],
                                self.bslice(5, 0, 512)[0:1, :])
            tk_ev = self.inc(ins, "sV")
            self.bank_free[4] = tk_ev
            self.bank_free[5] = tk_ev
            _ln_tail_tt(tt)
        tk_ln2e = [None] * self.n512
        for hh in range(self.nxh):
            for c in range(KC):
                pidx = hh * KC + c
                if pidx not in x_tok:
                    load_x_piece(pidx)
                slot = self.xtmp[pidx % 2]
                self.wait("v", *x_tok[pidx])
                pp = (self.lntB, self.combSB)[c % 2]
                for q in range(xw // 512):
                    tt0 = hh * xw + q * 512
                    src = self.bufa[:, c * t + tt0: c * t + tt0 + 512]
                    v.scalar_tensor_tensor(pp[:, :], src,
                                           self.g2ga[:, c:c + 1],
                                           self.urep[:, tt0:tt0 + 512],
                                           op0=ALU.mult, op1=ALU.mult)
                    v.scalar_tensor_tensor(pp[:, :],
                                           self.vrep[:, tt0:tt0 + 512],
                                           self.g2ga[:, c:c + 1], pp[:, :],
                                           op0=ALU.mult, op1=ALU.add)
                    ins = v.scalar_tensor_tensor(src, pp[:, :],
                                                 self.be2ga[:, c:c + 1],
                                                 slot[:, q * 512:(q + 1) * 512],
                                                 op0=ALU.add, op1=ALU.add)
                tk_res = self.inc(ins, "sV")
                self.xtmp_free[pidx % 2] = tk_res
                if pidx + 2 < self.nxh * KC:
                    load_x_piece(pidx + 2)
            tk_hh = (self.tok("sV"), self.tok("sV"))
            for q in range(xw // 512):
                tk_ln2e[hh * (xw // 512) + q] = tk_hh
        self.wait("g", self.tok("sV"))
        tk_shd = load_mat_16k((4, 5), self.i_shd)

        tk_apply2 = ln_sbuf(bufa_c, (self.bcol, self.sbcol), None,
                lambda c, tt0: self.b16[:, c * t + tt0: c * t + tt0 + 512],
                src_ready=tk_ln2e, xsq=self.g16)
        tk_h2 = self.tok("sV")

        def sh_lhs(slotpair, k, jcol):
            sl = self.exslot[slotpair[0] if k < 4 else slotpair[1]]
            kk = k % 4
            return sl[:, kk * KB + jcol: kk * KB + jcol + 128]
        shd_lhs = lambda k, jcol: sh_lhs((4, 5), k, jcol)

        self.wait("t", *tk_shg, *tk_shu, *tk_shd)
        self.wait("v", self.tok("sS"))
        prev_shh = None
        for tt in range(self.n512):
            self.wait("t", *tk_apply2[tt])
            rhs_h2 = [self.b16[:, k * t + tt * 512: k * t + (tt + 1) * 512]
                      for k in range(KC)]
            self.wait("s", *tk_apply2[tt])
            for hc in range(KC):
                bg = (4, 0)[hc % 2]
                bu = (5, 1)[hc % 2]
                self.bwait(bg); self.bwait(bu)
                for k in range(KC):
                    ins = self.mm(self.bank[bg], sh_lhs((0, 1), k, hc * 128),
                                  rhs_h2[k], start=(k == 0), stop=(k == KC - 1))
                tk_g8 = self.inc(ins, "sT")
                for k in range(KC):
                    ins = self.mm(self.bank[bu], sh_lhs((2, 3), k, hc * 128),
                                  rhs_h2[k], start=(k == 0), stop=(k == KC - 1))
                tk_gu = self.inc(ins, "sT")
                self.wait("s", tk_g8)
                if prev_shh is not None:
                    self.wait("s", prev_shh)
                ins = s.activation(self.g16[:, hc * 512:(hc + 1) * 512],
                                   self.bank[bg], AF.Silu)
                tk_sl = self.inc(ins, "sS")
                self.bank_free[bg] = tk_sl
                self.wait("v", tk_sl, tk_gu)
                if prev_shh is not None:
                    self.wait("v", prev_shh)
                ins = v.tensor_tensor(self.shh16[:, hc * 512:(hc + 1) * 512],
                                      self.bank[bu],
                                      self.g16[:, hc * 512:(hc + 1) * 512],
                                      op=ALU.mult)
                tk_shh = self.inc(ins, "sV")
                self.bank_free[bu] = tk_shh
            self.wait("t", tk_shh)
            for jc in range(KC):
                b = (2, 3, 6)[jc % 3]
                self.bwait(b)
                for k in range(KC):
                    ins = self.mm(self.bank[b], shd_lhs(k, jc * 128),
                                  self.shh16[:, k * 512:(k + 1) * 512],
                                  start=(k == 0), stop=(k == KC - 1))
                tk_sd = self.inc(ins, "sT")
                self.wait("v", tk_sd)
                dst = self.bufa[:, jc * t + tt * 512: jc * t + (tt + 1) * 512]
                ins = v.scalar_tensor_tensor(dst, self.bank[b],
                                             self.gbcol[:, jc:jc + 1], dst,
                                             op0=ALU.mult, op1=ALU.add)
                tk_se = self.inc(ins, "sV")
                self.bank_free[b] = tk_se
            prev_shh = ("sT", self.sems["sT"][1])

        exw = {}

        def load_expert(e, s_up, s_gt, s_dn, wait_tok):
            if wait_tok is not None:
                self.wait("g", wait_tok)
            self.wgrp_start()
            self.wdma(self.exslot[s_up][:, :],
                      self.i_up[e, :, :].rearrange("(k p) j -> p k j", p=128))
            self.wdma(self.exslot[s_gt][:, :],
                      self.i_gp[e, :, :].rearrange("(k p) j -> p k j", p=128))
            self.wdma(self.exslot[s_dn][:, :],
                      self.i_dp[e, :, :].rearrange("(k p) j -> p k j", p=128))
            exw[e] = (s_up, s_gt, s_dn, self.wgrp_end())

        nt = self.n128
        self.wait("t", tk_rwt)
        for i in range(nt):
            self.wait("t", *tk_apply2[(i * 128) // 512])
            b = (4, 5, 2, 3)[i % 4]
            self.bwait(b)
            for k in range(KC):
                ins = self.mm(self.bslice(b, 0, NE),
                              self.b16[:, k * t + i * 128: k * t + i * 128 + 128],
                              self.rwT[:, k * NE:(k + 1) * NE],
                              start=(k == 0), stop=(k == KC - 1))
            tk_lg = self.inc(ins, "sT")
            self.wait("v", tk_lg)
            ins = v.tensor_tensor(self.ltm[:, i * NE:(i + 1) * NE],
                                  self.bslice(b, 0, NE), self.rbrep[:, :],
                                  op=ALU.add)
            tk_lv = self.inc(ins, "sV")
            self.bank_free[b] = tk_lv
        lt3 = self.ltm.ap().rearrange("p (i e) -> p i e", e=NE)
        self.vfence(v.reduce_max(self.mx1[:, :], lt3, axis=AX))
        for i in range(nt):
            v.tensor_scalar(self.msk1[:, i * NE:(i + 1) * NE],
                            self.ltm[:, i * NE:(i + 1) * NE],
                            self.mx1[:, i:i + 1], None, op0=ALU.is_equal)
        for i in range(nt):
            v.scalar_tensor_tensor(self.msk2[:, i * NE:(i + 1) * NE],
                                   self.msk1[:, i * NE:(i + 1) * NE], -1e30,
                                   self.ltm[:, i * NE:(i + 1) * NE],
                                   op0=ALU.mult, op1=ALU.add)
        m23 = self.msk2.ap().rearrange("p (i e) -> p i e", e=NE)
        self.vfence(v.reduce_max(self.mx2[:, :], m23, axis=AX))
        for i in range(nt):
            v.tensor_scalar(self.msk2[:, i * NE:(i + 1) * NE],
                            self.msk2[:, i * NE:(i + 1) * NE],
                            self.mx2[:, i:i + 1], None, op0=ALU.is_equal)
        ins = v.tensor_tensor(self.s1a[:, :], self.mx1[:, :], self.mx2[:, :],
                              op=ALU.subtract)
        tk_dd = self.inc(ins, "sV")
        self.wait("s", tk_dd)
        ins = s.activation(self.s1a[:, :], self.s1a[:, :], AF.Sigmoid)
        tk_sg = self.inc(ins, "sS")
        self.wait("v", tk_sg)
        v.tensor_scalar(self.s2a[:, :], self.s1a[:, :], -1.0, 1.0,
                        op0=ALU.mult, op1=ALU.add)
        for i in range(nt):
            v.tensor_scalar(self.msk1[:, i * NE:(i + 1) * NE],
                            self.msk1[:, i * NE:(i + 1) * NE],
                            self.s1a[:, i:i + 1], None, op0=ALU.mult)
        for i in range(nt):
            v.scalar_tensor_tensor(self.msk2[:, i * NE:(i + 1) * NE],
                                   self.msk2[:, i * NE:(i + 1) * NE],
                                   self.s2a[:, i:i + 1],
                                   self.msk1[:, i * NE:(i + 1) * NE],
                                   op0=ALU.mult, op1=ALU.add)
        ins = v.tensor_copy(self.comb16[:, :], self.msk2[:, :])
        tk_cmb = self.inc(ins, "sV")
        tk_tp_ev = None
        self.wait("t", tk_cmb)
        nw = (nt + 7) // 8
        for w in range(nw):
            if tk_tp_ev is not None:
                self.wait("t", tk_tp_ev)
            lo = w * 8
            hi = min(nt, lo + 8)
            for i in range(lo, hi):
                ins = te.transpose(self.pstb[0:16, (i - lo) * 128:(i - lo + 1) * 128],
                                   self.comb16[:, i * NE:(i + 1) * NE],
                                   self.id128[:, :])
            tk_tp = self.inc(ins, "sT")
            self.wait("v", tk_tp)
            ins = v.tensor_copy(self.combine_fm[:, lo * 128: hi * 128],
                                self.pstb[0:16, 0:(hi - lo) * 128])
            tk_tp_ev = self.inc(ins, "sV")

        self.dump("h2", self.b16[:, :])
        self.dump("ltm", self.ltm[:, :])
        self.dump("combine_fm", self.combine_fm[:, :])
        tk_post_shared = self.tok("sT")
        self.wait("g", self.tok("sT"), self.tok("sV"))
        load_expert(0, 0, 1, 2, None)
        load_expert(1, 3, 4, 5, None)
        prev_hid = {0: tk_post_shared, 1: tk_post_shared}
        hsel = 0
        comb_bufs = [self.comb_all, self.comb_allB]
        comb_tok = {}

        def emit_comb_unit(e, tt):
            dstc = comb_bufs[e % 2]
            self.wait("t", tk_tp_ev)
            self.bwait(4)
            ins = self.mm(self.bank[4], self.selbank[:, e * 128:(e + 1) * 128],
                          self.combine_fm[:, tt * 512:(tt + 1) * 512],
                          True, True)
            tk_cr = self.inc(ins, "sT")
            self.wait("v", tk_cr)
            ins = v.tensor_copy(dstc[:, tt * 512:(tt + 1) * 512], self.bank[4])
            tk_cb = self.inc(ins, "sV")
            self.bank_free[4] = tk_cb
            if tt == self.n512 - 1:
                comb_tok[e] = tk_cb

        def emit_comb(e):
            for tt in range(self.n512):
                emit_comb_unit(e, tt)

        emit_comb(0)

        def emit_downs(pd):
            e_, tt_, hid_, s_dn_, tk_hid_ = pd
            for jc in range(KC):
                b = 2 + jc % 2
                self.bwait(b)
                for kk in range(4):
                    self.wait("t", tk_hid_[kk])
                    ins = self.mm(self.bank[b],
                                  self.exslot[s_dn_][:, kk * KB + jc * 128: kk * KB + jc * 128 + 128],
                                  hid_[:, kk * 512:(kk + 1) * 512],
                                  start=(kk == 0), stop=(kk == 3))
                tk_dn = self.inc(ins, "sT")
                self.wait("v", tk_dn)
                dst = self.bufa[:, jc * t + tt_ * 512: jc * t + (tt_ + 1) * 512]
                ins = v.scalar_tensor_tensor(dst, self.bank[b],
                                             self.gbcol[:, jc:jc + 1], dst,
                                             op0=ALU.mult, op1=ALU.add)
                tk_de = self.inc(ins, "sV")
                self.bank_free[b] = tk_de
                if e_ == NE - 1:
                    self.wait("y", tk_de)
                    self.dma("y",
                             self.o_y[jc * 128:(jc + 1) * 128,
                                      tt_ * 512:(tt_ + 1) * 512],
                             self.bufa[:, jc * t + tt_ * 512:
                                       jc * t + (tt_ + 1) * 512],
                             ("sOut", "sOut1", "sOut2")[jc % 3])

        pending = None
        for e in range(NE):
            s_up, s_gt, s_dn, tk_w = exw[e]
            self.wait("t", *tk_w)
            comb = comb_bufs[e % 2]
            tk_comb = comb_tok[e]
            for tt in range(self.n512):
                rhs_h2 = [self.b16[:, k * t + tt * 512: k * t + (tt + 1) * 512]
                          for k in range(KC)]
                hid = self.hid16r[hsel]
                tk_hid = [None] * 4
                for ec in range(4):
                    bg = (5, 0)[ec % 2]
                    bu = (6, 1)[ec % 2]
                    self.bwait(bg); self.bwait(bu)
                    for k in range(KC):
                        ins = self.mm(self.bank[bg],
                                      self.exslot[s_gt][:, k * 512 + ec * 128: k * 512 + ec * 128 + 128],
                                      rhs_h2[k], start=(k == 0), stop=(k == KC - 1))
                    tk_g8 = self.inc(ins, "sT")
                    for k in range(KC):
                        ins = self.mm(self.bank[bu],
                                      self.exslot[s_up][:, k * 512 + ec * 128: k * 512 + ec * 128 + 128],
                                      rhs_h2[k], start=(k == 0), stop=(k == KC - 1))
                    tk_gu = self.inc(ins, "sT")
                    if tt == 0 and e + 1 < NE and self.n512 == 4:
                        emit_comb_unit(e + 1, ec)
                    self.wait("s", tk_g8)
                    if prev_hid.get(hsel) is not None:
                        self.wait("s", prev_hid[hsel])
                    ins = s.activation(self.g16[:, ec * 512:(ec + 1) * 512],
                                       self.bank[bg], AF.Silu)
                    tk_sl = self.inc(ins, "sS")
                    self.bank_free[bg] = tk_sl
                    self.wait("v", tk_gu, tk_comb)
                    if prev_hid.get(hsel) is not None:
                        self.wait("v", prev_hid[hsel])
                    ins = v.tensor_tensor(hid[:, ec * 512:(ec + 1) * 512],
                                          self.bank[bu],
                                          comb[:, tt * 512:(tt + 1) * 512],
                                          op=ALU.mult)
                    self.inc(ins, "sV")
                    self.bank_free[bu] = self.tok("sV")
                    self.wait("v", tk_sl)
                    ins = v.tensor_tensor(hid[:, ec * 512:(ec + 1) * 512],
                                          hid[:, ec * 512:(ec + 1) * 512],
                                          self.g16[:, ec * 512:(ec + 1) * 512],
                                          op=ALU.mult)
                    tk_hid[ec] = self.inc(ins, "sV")
                if pending is not None:
                    pe = pending[0]
                    emit_downs(pending)
                    prev_hid[hsel ^ 1] = self.tok("sT")
                    if pe != e and pe + 2 < NE:
                        load_expert(pe + 2, *exw[pe][:3], self.tok("sT"))
                pending = (e, tt, hid, s_dn, tk_hid)
                hsel ^= 1
                if tt == 0 and e + 1 < NE and self.n512 != 4:
                    emit_comb(e + 1)
        emit_downs(pending)

        self.wait("y", self.tok("sOut"), self.tok("sOut1"),
                  self.tok("sOut2"))
        return nc


def _col(vec, nchunk):
    return np.ascontiguousarray(np.asarray(vec, np.float32).reshape(nchunk, 128).T)


def _prep_shared(inp):
    d = {}
    d["Wq"] = np.ascontiguousarray(inp["L0_attn_Wq"], np.float32)
    d["Wk"] = np.ascontiguousarray(inp["L0_attn_Wk"], np.float32)
    d["Wv"] = np.ascontiguousarray(inp["L0_attn_Wv"], np.float32)
    d["Wo"] = np.ascontiguousarray(inp["L0_attn_Wo"], np.float32)
    d["W1"] = np.ascontiguousarray(inp["L0_attn_W1"], np.float32)
    d["W2"] = np.ascontiguousarray(inp["L0_attn_W2"], np.float32)
    d["bq_col"] = _col(inp["L0_attn_bq"], KC)
    d["bo_col"] = _col(inp["L0_attn_bo"], KC)
    d["b1_col"] = _col(inp["L0_attn_b1"], 32)
    d["b2_col"] = _col(inp["L0_attn_b2"], KC)
    d["bk_row"] = np.ascontiguousarray(
        np.asarray(inp["L0_attn_bk"], np.float32).reshape(1, D))
    d["bv_row"] = np.ascontiguousarray(
        np.asarray(inp["L0_attn_bv"], np.float32).reshape(1, D))
    d["g1_col"] = _col(inp["L0_attn_g1"], KC)
    d["be1_col"] = _col(inp["L0_attn_beta1"], KC)
    d["g2_col"] = _col(inp["L0_attn_g2"], KC)
    d["be2_col"] = _col(inp["L0_attn_beta2"], KC)
    d["rwT"] = np.ascontiguousarray(
        np.asarray(inp["L0_ffn_router_w"], np.float32).T.reshape(KC, 128, NE)
        .transpose(1, 0, 2).reshape(128, KC * NE))
    d["rb_row"] = np.ascontiguousarray(
        np.asarray(inp["L0_ffn_router_bias"], np.float32).reshape(1, NE))
    d["up_p"] = np.ascontiguousarray(inp["L0_ffn_up_proj"], np.float32)
    d["gate_p"] = np.ascontiguousarray(inp["L0_ffn_gate_proj"], np.float32)
    d["down_p"] = np.ascontiguousarray(inp["L0_ffn_down_proj"], np.float32)
    d["shgT"] = np.ascontiguousarray(np.asarray(inp["L0_ffn_sh_gate"], np.float32).T)
    d["shuT"] = np.ascontiguousarray(np.asarray(inp["L0_ffn_sh_up"], np.float32).T)
    d["shdT"] = np.ascontiguousarray(np.asarray(inp["L0_ffn_sh_down"], np.float32).T)
    d["onesf"] = np.ones((1, 128), np.float32)
    d["ones128b"] = np.ones((128, 1), BF)
    d["ones128f"] = np.ones((128, 1), np.float32)
    esel = np.zeros((H, D), np.float32)
    for h in range(H):
        esel[h, h * DH:(h + 1) * DH] = 1.0
    d["esel"] = esel.astype(BF)
    selb = np.zeros((NE, NE * 128), np.float32)
    for e in range(NE):
        selb[e, e * 128:(e + 1) * 128] = 1.0
    d["selbank"] = selb.astype(BF)
    d["id128"] = np.eye(128, dtype=np.float32).astype(BF)
    return d


_PROG_CACHE = {}


def _get_prog(t, ncores):
    key = (t, ncores)
    if key not in _PROG_CACHE:
        p = Prog(t=t, ncores=ncores)
        p.build()
        _PROG_CACHE[key] = p
    return _PROG_CACHE[key]


def kernel(**inputs):
    t = 2048
    ncores = 8
    p = _get_prog(t, ncores)
    shared = _prep_shared(inputs)
    x = np.asarray(inputs["L0_attn_x"], np.float32)[0]
    xT = np.ascontiguousarray(x.T)
    cvec = np.asarray(inputs["L0_c"], np.float32).reshape(D)
    wmod = np.ascontiguousarray(inputs["L0_wmod"], np.float32)
    rpc = D // ncores
    in_maps = []
    for c in range(ncores):
        m = dict(shared)
        m["xT"] = np.ascontiguousarray(xT[:, c * t:(c + 1) * t])
        m["c_col"] = _col(cvec[c * rpc:(c + 1) * rpc], rpc // 128)
        m["wmods"] = np.ascontiguousarray(wmod[c * rpc:(c + 1) * rpc, :])
        in_maps.append(m)
    res = run_bass_kernel_spmd(p.nc, in_maps, core_ids=list(range(ncores)))
    parts = [np.asarray(res.results[c]["y"]).T for c in range(ncores)]
    out = np.concatenate(parts, axis=0).reshape(1, ncores * t, D)
    return np.ascontiguousarray(out, dtype=np.float32)


# version 32
# speedup vs baseline: 1.0065x; 1.0065x over previous
import numpy as np
import ml_dtypes
import concourse.bass as bass
import concourse.mybir as mybir
from concourse.bass_utils import run_bass_kernel_spmd

F32 = mybir.dt.float32
BF16 = mybir.dt.bfloat16
U8 = mybir.dt.uint8
AF = mybir.ActivationFunctionType
ALU = mybir.AluOpType
AX = mybir.AxisListType.X
BF = ml_dtypes.bfloat16

D = 1024
KC = 8
H = 16
DH = 64
DFF = 4096
NE = 16
DE = 512
EPS_LN = 1e-5
EPS_ATTN = 1e-6

KB = 1024


class Prog:

    def __init__(self, t=2048, ncores=8, dbg=False):
        self.dbg = dbg
        self.dbg_t = {}
        assert t % 512 == 0
        self.t = t
        self.n512 = t // 512
        self.n128 = t // 128
        self.ncores = ncores
        self.xw = min(1024, t)
        self.nxh = t // self.xw
        nc = bass.Bass("TRN2", target_bir_lowering=False, debug=False,
                       detect_race_conditions=False)
        self.nc = nc
        self.eng = {"t": nc.tensor, "v": nc.vector, "s": nc.scalar,
                    "g": nc.gpsimd, "y": nc.sync}
        self.sems = {}
        for nm in ["sW0", "sW1", "sW2", "sW3", "sX0", "sX1", "sT", "sV", "sS",
                   "sCC", "sOut", "sOut1", "sOut2", "sG"]:
            h = nc.alloc_semaphore(name=nm)
            self.sems[nm] = [h, 0]
        self.last_wait = {}
        self.sems["sDbg"] = [nc.alloc_semaphore(name="sDbg"), 0]
        self._wrr = 0
        self._wgrp = {}
        self.declare_io()
        self.alloc_mem()

    def inc(self, ins, sem, k=1):
        rec = self.sems[sem]
        ins.then_inc(rec[0], k)
        rec[1] += k
        return (sem, rec[1])

    def tok(self, sem):
        return (sem, self.sems[sem][1])

    def wait(self, ename, *tokens):
        eng = self.eng[ename]
        for tk in tokens:
            if tk is None:
                continue
            sem, val = tk
            if val <= 0:
                continue
            key = (ename, sem)
            if self.last_wait.get(key, 0) < val:
                eng.wait_ge(self.sems[sem][0], val)
                self.last_wait[key] = val

    def declare_io(self):
        nc = self.nc
        t = self.t

        def din(name, shape, dt=F32):
            return nc.dram_tensor(name, shape, dt, kind="ExternalInput").ap()

        self.i_xT = din("xT", [D, t])
        self.i_Wq = din("Wq", [D, D]); self.i_Wk = din("Wk", [D, D])
        self.i_Wv = din("Wv", [D, D]); self.i_Wo = din("Wo", [D, D])
        self.i_W1 = din("W1", [D, DFF]); self.i_W2 = din("W2", [DFF, D])
        self.i_bqc = din("bq_col", [128, KC]); self.i_boc = din("bo_col", [128, KC])
        self.i_b1c = din("b1_col", [128, 32]); self.i_b2c = din("b2_col", [128, KC])
        self.i_bkr = din("bk_row", [1, D]); self.i_bvr = din("bv_row", [1, D])
        self.i_g1c = din("g1_col", [128, KC]); self.i_be1c = din("be1_col", [128, KC])
        self.i_g2c = din("g2_col", [128, KC]); self.i_be2c = din("be2_col", [128, KC])
        kpc = D // (128 * self.ncores)
        assert kpc >= 1 and D % (128 * self.ncores) == 0
        self.kpc = kpc
        self.i_ccol = din("c_col", [128, kpc])
        self.i_wmod = din("wmods", [D // self.ncores, 6 * D])
        self.i_rwT = din("rwT", [128, KC * NE]); self.i_rbr = din("rb_row", [1, NE])
        self.i_up = din("up_p", [NE, D, DE]); self.i_gp = din("gate_p", [NE, D, DE])
        self.i_dp = din("down_p", [NE, DE, D])
        self.i_shg = din("shgT", [D, D]); self.i_shu = din("shuT", [D, D])
        self.i_shd = din("shdT", [D, D])
        self.i_onesf = din("onesf", [1, 128])
        self.i_ones128b = din("ones128b", [128, 1], BF16)
        self.i_ones128f = din("ones128f", [128, 1])
        self.i_esel = din("esel", [H, D], BF16)
        self.i_selbank = din("selbank", [NE, NE * 128], BF16)
        self.i_id128 = din("id128", [128, 128], BF16)
        self.o_y = nc.dram_tensor("y", [D, t], F32, kind="ExternalOutput").ap()
        self.cc_in = nc.dram_tensor("cc_in", [128, 520], F32)
        self.cc_out = nc.dram_tensor("cc_out", [128, 520], F32)
        self.cc_min = nc.dram_tensor("cc_min", [128, 48], F32)
        self.cc_mout = nc.dram_tensor("cc_mout", [128, 48], F32)

    def alloc_mem(self):
        nc = self.nc
        t = self.t
        tb2 = t * 2
        tb4 = t * 4
        ARENA = 206 * KB
        self.arena = nc.alloc_sbuf_tensor("arena", [128, ARENA], U8)
        base = nc.lookup_mloc(self.arena).addr

        def at(name, shape, dt, off):
            per = int(np.prod(shape[1:])) * mybir.dt.size(dt)
            assert off % 32 == 0, (name, off)
            assert off + per <= ARENA, (name, off, per)
            return nc.alloc_sbuf_tensor_at(name, list(shape), dt, offset=base + off)

        o = 0
        self.onesf = at("onesf_sb", [1, 128], F32, o); o += 512
        self.ones128b = at("ones128b_sb", [128, 1], BF16, o); o += 32
        self.ones128f = at("ones128f_sb", [128, 1], F32, o); o += 32
        self.id128 = at("id128_sb", [128, 128], BF16, o); o += 256
        self.bqc = at("bqc", [128, KC], F32, o); o += 32
        self.boc = at("boc", [128, KC], F32, o); o += 32
        self.b1c = at("b1c", [128, 32], F32, o); o += 128
        self.b2c = at("b2c", [128, KC], F32, o); o += 32
        self.g1c = at("g1c", [128, KC], F32, o); o += 32
        self.be1c = at("be1c", [128, KC], F32, o); o += 32
        self.g2c = at("g2c", [128, KC], F32, o); o += 32
        self.be2c = at("be2c", [128, KC], F32, o); o += 32
        self.ccol = at("ccol", [128, KC], F32, o); o += 32
        self.ccolS = at("ccolS", [128, KC], F32, o); o += 32
        self.modcol = at("modcol", [128, 48], F32, o); o += 192
        self.acol = at("acol", [128, KC], F32, o); o += 32
        self.sacol = at("sacol", [128, KC], F32, o); o += 32
        self.gacol = at("gacol", [128, KC], F32, o); o += 32
        self.bcol = at("bcol", [128, KC], F32, o); o += 32
        self.sbcol = at("sbcol", [128, KC], F32, o); o += 32
        self.gbcol = at("gbcol", [128, KC], F32, o); o += 32
        self.g2ga = at("g2ga", [128, KC], F32, o); o += 32
        self.be2ga = at("be2ga", [128, KC], F32, o); o += 32
        self.esel = at("esel_sb", [H, D], BF16, o); o += 2 * KB
        self.selbank = at("selbank_sb", [NE, NE * 128], BF16, o); o += 4 * KB
        self.rwT = at("rwT_sb", [128, KC * NE], BF16, o); o += 256
        self.rbrep = at("rbrep", [128, NE], F32, o); o += 64
        self.combSB = at("combSB", [128, 512], F32, o); o += 2 * KB
        self.zcol = at("zcol", [128, 1], F32, o); o += 32
        o = (o + 31) // 32 * 32
        oA = o
        self.bkrep = at("bkrep", [128, D], F32, oA)
        self.bvrep = at("bvrep", [128, D], F32, oA + 4 * KB)
        self.kvacc = at("kvacc", [128, 8 * 65], F32, oA + 8 * KB)
        self.kvg = at("kvg", [128, 8 * 65], F32, oA + 10 * KB + 96)
        self.kvg16 = at("kvg16", [128, 8 * 65], BF16, oA + 12 * KB + 192)
        self.ksdf = at("ksdf", [128, 128], F32, oA + 13 * KB + 256)
        self.ksd16 = at("ksd16", [128, 128], BF16, oA + 13 * KB + 768)
        self.wo_a = at("wo_a", [128, KC * 512], BF16, oA)
        self.hid16r = [at("hid16a", [128, 4 * 512], BF16, oA),
                       at("hid16b", [128, 4 * 512], BF16, oA + 4 * KB)]
        self.shh16 = at("shh16", [128, 8 * 512], BF16, oA + 4 * KB)
        self.comb_all = at("comb_all", [128, 2048], BF16, oA + 8 * KB)
        o = oA + 14 * KB

        oP2 = o
        self.den = at("den", [16, 512], F32, oP2)
        self.zrow = at("zrow", [16, 512], F32, oP2 + 2 * KB)
        self.z16r = [at("z16a", [16, 512], BF16, oP2 + 4 * KB),
                     at("z16b", [16, 512], BF16, oP2 + 5 * KB)]
        self.zrep = at("zrep", [128, 512], F32, oP2 + 6 * KB)
        nt = self.n128
        self.combine_fm = at("combine_fm", [NE, t], BF16, oP2)
        oF = oP2 + 4 * KB
        self.ltm = at("ltm", [128, NE * nt], F32, oF)
        oF2 = oF + 1 * KB
        nt4 = (nt * 4 + 31) // 32 * 32
        self.mx1 = at("mx1", [128, nt], F32, oF2)
        self.mx2 = at("mx2", [128, nt], F32, oF2 + nt4)
        self.s1a = at("s1a", [128, nt], F32, oF2 + 2 * nt4)
        self.s2a = at("s2a", [128, nt], F32, oF2 + 3 * nt4)
        oF3 = oF2 + 4 * nt4
        self.msk1 = at("msk1", [128, NE * nt], F32, oF3)
        self.msk2 = at("msk2", [128, NE * nt], F32, oF3 + NE * nt * 4)
        self.comb16 = at("comb16", [128, NE * nt], BF16, oF3 + 2 * NE * nt * 4)
        assert oF3 + 2 * NE * nt * 4 + NE * nt * 2 <= oP2 + 8 * KB
        self.g16 = at("g16", [128, 8 * 512], BF16, oP2 + 8 * KB)
        o = oP2 + max(2 * tb4, 16 * KB)

        oP3 = o
        self.urep = at("urep", [128, t], F32, oP3)
        self.vrep = at("vrep", [128, t], F32, oP3 + tb4)
        self.rowA = at("rowA", [1, t], F32, oP3)
        self.rowB = at("rowB", [1, t], F32, oP3 + tb4)
        self.shdT = at("shdT_sb", [128, KC * KB], BF16, oP3)
        o = oP3 + max(2 * tb4, 16 * KB)

        self.b16 = at("b16", [128, KC * t], BF16, o); o += KC * tb2

        oBA = o
        self.bufa = at("bufa", [128, KC * t], F32, oBA)
        self.wmod_sb = at("wmod_sb", [128, KC * 512], F32, oBA)
        self.wk = at("wk_sb", [128, KC * KB], BF16, oBA + 16 * KB)
        self.wv = at("wv_sb", [128, KC * KB], BF16, oBA + 32 * KB)
        self.ketmp = at("ketmp", [128, D], F32, oBA + 48 * KB)
        self.k16r = [at("k16a", [128, D], BF16, oBA + 52 * KB),
                     at("k16b", [128, D], BF16, oBA + 54 * KB)]
        self.vsbr = [at("vsba", [128, H * 65], BF16, oBA + 56 * KB),
                     at("vsbb", [128, H * 65], BF16, oBA + 58 * KB + 512)]
        self.tmpA = at("tmpA", [128, 512], F32, oBA)
        self.tmpB = at("tmpB", [128, 512], F32, oBA + 2 * KB)
        self.tmpC = at("tmpC", [128, 512], F32, oBA + 4 * KB)
        self.tmpD = at("tmpD", [128, 512], F32, oBA + 6 * KB)
        self.xtmp2 = [at("xtmp2_0", [128, 1024], F32, oBA),
                      at("xtmp2_1", [128, 1024], F32, oBA + 4 * KB)]
        self.tmpE = at("tmpE", [128, 512], F32, oBA + 8 * KB)
        o = oBA + max(KC * tb4, 64 * KB)

        self.lntA = at("lntA", [128, 512], F32, o)
        self.comb_allB = at("comb_allB", [128, 2048], BF16, o)
        o += 2 * KB
        self.lntB = at("lntB", [128, 512], F32, o); o += 2 * KB

        oR2 = o
        self.xsq16 = at("xsq16", [128, KC * t], BF16, oR2)
        self.qp = at("qp", [128, KC * t], BF16, oR2)
        self.ffnslot = [at(f"ffns{i}", [128, KC * 512], BF16, oR2 + i * 8 * KB)
                        for i in range(4)]
        self.xsqF = at("xsqF", [128, KC * 512], BF16, oR2)
        self.exslot = [at(f"exs{i}", [128, KC * 512], BF16, oR2 + i * 8 * KB)
                       for i in range(4)]
        o = oR2 + max(KC * tb2, 32 * KB)

        oR4 = o
        self.xtmp = [at(f"xtmp{i}", [128, 1024], F32, oR4 + i * 4 * KB)
                     for i in range(2)]
        self.wq = at("wq_sb", [128, KC * KB], BF16, oR4)
        self.attn16 = at("attn16", [128, KC * 512], BF16, oR4)
        self.wo_b = at("wo_b", [128, KC * 512], BF16, oR4 + 8 * KB)
        self.xsqT = at("xsqT", [128, KC * 512], BF16, oR4)
        self.relu16 = at("relu16", [128, 8 * 512], BF16, oR4 + 8 * KB)
        self.exslot.append(at("exs4", [128, KC * 512], BF16, oR4))
        self.exslot.append(at("exs5", [128, KC * 512], BF16, oR4 + 8 * KB))
        o = oR4 + 16 * KB
        assert o <= ARENA, o

        self.ps = self.nc.alloc_psum_tensor("ps", [128, 7 * 512], F32)
        self.pstb = self.nc.alloc_psum_tensor("pstb", [128, 1024], BF16)
        self.bank = [self.ps[:, i * 512:(i + 1) * 512] for i in range(7)]
        self.bank_free = [None] * 8

    def bslice(self, b, lo, hi):
        return self.ps[:, b * 512 + lo: b * 512 + hi]

    def dump(self, name, sbuf_ap):
        if not self.dbg:
            return
        shape = list(sbuf_ap.shape)
        dt = sbuf_ap.dtype
        h = self.nc.dram_tensor("dbg_" + name, shape, dt, kind="ExternalOutput").ap()
        self.dbg_t[name] = (shape, dt)
        self.wait("y", self.tok("sV"), self.tok("sS"), self.tok("sT"),
                  self.tok("sG"))
        tk = self.dma("y", h, sbuf_ap, "sDbg")
        for e in ("t", "v", "s", "g"):
            self.wait(e, tk)

    def mm(self, out_ap, lhsT, rhs, start, stop, tile_position=None):
        return self.nc.tensor.matmul(out_ap, lhsT, rhs, start=start, stop=stop,
                                     tile_position=tile_position)

    def dma(self, q, out, in_, sem):
        self.wait(q, self.tok(sem))
        ins = self.eng[q].dma_start(out=out, in_=in_)
        return self.inc(ins, sem, 16)

    def wgrp_start(self):
        self._wgrp = {}

    def wdma(self, out, in_):
        sem = f"sW{self._wrr}"
        self._wrr = (self._wrr + 1) % 4
        tk = self.dma("g", out, in_, sem)
        self._wgrp[tk[0]] = tk[1]
        return tk

    def wgrp_end(self):
        return [ (k, v) for k, v in self._wgrp.items() ]

    def vfence(self, ins):
        tk = self.inc(ins, "sV")
        self.wait("v", tk)
        return tk

    def bwait(self, b):
        bf = self.bank_free[b]
        if bf is None:
            return
        if isinstance(bf, list):
            self.wait("t", *bf)
        else:
            self.wait("t", bf)

    def build(self):
        t = self.t
        nc = self.nc
        v, s, te, g, y = nc.vector, nc.scalar, nc.tensor, nc.gpsimd, nc.sync
        self.xtmp_free = [None, None]
        self._x_piece = {}
        self._x_cast = set()
        self._xtmp_vtok = ("sV", 0)

        for c_pre in (0, 1):
            slot = self.xtmp[c_pre % 2]
            h0 = self.dma("y", slot[0:64, 0:self.xw],
                          self.i_xT[c_pre * 128:c_pre * 128 + 64, 0:self.xw],
                          "sX0")
            h1 = self.dma("y", slot[64:128, 0:self.xw],
                          self.i_xT[c_pre * 128 + 64:(c_pre + 1) * 128, 0:self.xw],
                          "sX1")
            self._x_piece[(0, c_pre)] = (h0, h1)
        for dst, src in [
            (self.onesf, self.i_onesf), (self.ones128b, self.i_ones128b),
            (self.ones128f, self.i_ones128f),
            (self.id128, self.i_id128), (self.bqc, self.i_bqc),
            (self.boc, self.i_boc), (self.b1c, self.i_b1c), (self.b2c, self.i_b2c),
            (self.g1c, self.i_g1c), (self.be1c, self.i_be1c),
            (self.g2c, self.i_g2c), (self.be2c, self.i_be2c),
            (self.esel, self.i_esel),
            (self.selbank, self.i_selbank),
        ]:
            tk_consts = self.dma("y", dst[:, :], src, "sX0")
        tk_consts = self.dma("y", self.ccol[:, 0:self.kpc], self.i_ccol, "sX0")
        tk_rwt = self.wdma(self.rwT[:, :], self.i_rwT)

        wtk = {}
        for nm, dst, srcw in [("wk", self.wk, self.i_Wk), ("wv", self.wv, self.i_Wv)]:
            s3 = srcw.rearrange("(k p) j -> p k j", p=128)
            d3 = dst.ap().rearrange("p (k j) -> p k j", j=KB)
            wtk[nm] = [self.wdma(d3[:, 0:4, :], s3[:, 0:4, :]),
                       self.wdma(d3[:, 4:8, :], s3[:, 4:8, :])]

        v.memset(self.zcol[:, :], 0.0)
        self.wait("s", tk_consts)
        ins = s.activation(self.ccolS[:, 0:self.kpc], self.ccol[:, 0:self.kpc],
                           AF.Silu)
        tk_silu_c = self.inc(ins, "sS")
        psmod = self.bslice(6, 0, 48)
        kpc = self.kpc
        pw = 1024
        wm3 = self.wmod_sb.ap().rearrange("p (k j) -> p k j", j=pw)
        npc = 6 * D // pw
        for js in range(npc):
            tk_wm = self.wdma(wm3[:, 0:kpc, :],
                              self.i_wmod.rearrange("(k p) j -> p k j", p=128)[
                                  :, :, js * pw:(js + 1) * pw])
            self.wait("t", tk_silu_c, tk_wm)
            for jc in range(pw // 128):
                jg = js * (pw // 128) + jc
                for k in range(kpc):
                    ins = self.mm(psmod[:, jg:jg + 1],
                                  self.wmod_sb[:, k * pw + jc * 128: k * pw + jc * 128 + 128],
                                  self.ccolS[:, k:k + 1],
                                  start=(k == 0), stop=(k == kpc - 1))
            tk_mod_mm = self.inc(ins, "sT")
            if js < npc - 1:
                self.wait("g", tk_mod_mm)
        self.wait("v", tk_mod_mm, tk_consts)
        self.vfence(v.tensor_copy(self.modcol[:, :], psmod))
        self.wait("y", self.tok("sV"))
        tk_mcin = self.dma("y", self.cc_min.ap(), self.modcol[:, :], "sX0")
        self.wait("g", tk_mcin)
        ins = g.collective_compute(
            "AllReduce", ALU.add,
            replica_groups=[list(range(self.ncores))],
            ins=[self.cc_min.ap().opt()], outs=[self.cc_mout.ap().opt()])
        tk_mcc = self.inc(ins, "sCC")
        self.wait("g", tk_mcc)
        tk_mcb = self.dma("g", self.modcol[:, :], self.cc_mout.ap(), "sW3")

        def emit_mod_derive():
            self.wait("v", tk_mcb, tk_consts)
            mc = self.modcol
            v.tensor_copy(self.sacol[:, :], mc[:, 0:8])
            v.tensor_scalar(self.acol[:, :], mc[:, 8:16], 1.0, None, op0=ALU.add)
            v.tensor_copy(self.sbcol[:, :], mc[:, 24:32])
            v.tensor_scalar(self.bcol[:, :], mc[:, 32:40], 1.0, None, op0=ALU.add)
            for src_lo, dst in [(16, self.gacol), (40, self.gbcol)]:
                gsl = mc[:, src_lo:src_lo + 8]
                p1 = self.lntA[:, 0:8]; n1 = self.lntA[:, 8:16]
                mk = self.lntA[:, 16:24]
                v.tensor_scalar(p1, gsl, 0.25, None, op0=ALU.max)
                v.tensor_scalar(n1, gsl, -0.25, None, op0=ALU.min)
                self.vfence(v.tensor_scalar(mk, gsl, self.zcol[:, 0:1], None,
                                            op0=ALU.is_ge))
                v.tensor_tensor(p1, p1, mk, op=ALU.mult)
                self.vfence(v.tensor_scalar(mk, mk, -1.0, 1.0,
                                            op0=ALU.mult, op1=ALU.add))
                self.vfence(v.tensor_tensor(n1, n1, mk, op=ALU.mult))
                ins = v.tensor_tensor(dst[:, :], p1, n1, op=ALU.add)
            self.inc(ins, "sV")
            v.tensor_tensor(self.g2ga[:, :], self.g2c[:, :], self.gacol[:, :],
                            op=ALU.mult)
            v.tensor_tensor(self.be2ga[:, :], self.be2c[:, :], self.gacol[:, :],
                            op=ALU.mult)

        vs3r = [sb.ap().rearrange("p (h m) -> p h m", m=65) for sb in self.vsbr]
        for vs3 in vs3r:
            v.memset(vs3[:, :, 64:65], 1.0)

        def _lap(src_ap, c, tt0, sc_col, bi_col, out_fp32, out_bf16, gp=False):
            pp = (self.lntB, self.combSB)[c % 2]
            v.scalar_tensor_tensor(pp[:, :], src_ap, sc_col[:, c:c + 1],
                                   self.urep[:, tt0:tt0 + 512],
                                   op0=ALU.mult, op1=ALU.mult)
            ins = v.scalar_tensor_tensor(pp[:, :], self.vrep[:, tt0:tt0 + 512],
                                         sc_col[:, c:c + 1], pp[:, :],
                                         op0=ALU.mult, op1=ALU.add)
            self.inc(ins, "sV")
            if out_fp32 is not None:
                ins = v.tensor_scalar(out_fp32(c, tt0), pp[:, :],
                                      bi_col[:, c:c + 1], None, op0=ALU.add)
                tkf = self.inc(ins, "sV")
                if out_bf16 is not None:
                    self.wait("s", tkf)
                    ins = s.copy(out_bf16(c, tt0), out_fp32(c, tt0))
                    self.inc(ins, "sS")
            elif out_bf16 is not None:
                ins = v.tensor_scalar(out_bf16(c, tt0), pp[:, :],
                                      bi_col[:, c:c + 1], None, op0=ALU.add)
                tkf = self.inc(ins, "sV")

        def _ln_tail_tt(tt):
            sl = slice(tt * 512, (tt + 1) * 512)
            ra, rb = self.rowA[0:1, sl], self.rowB[0:1, sl]
            rt = self.lntA[0:1, 0:512]
            v.tensor_scalar(ra, ra, 1.0 / D, None, op0=ALU.mult)
            v.tensor_scalar(rb, rb, 1.0 / D, None, op0=ALU.mult)
            v.tensor_tensor(rt, ra, ra, op=ALU.mult)
            v.tensor_tensor(rb, rb, rt, op=ALU.subtract)
            ins = v.tensor_scalar(rb, rb, EPS_LN, None, op0=ALU.add)
            tk_var = self.inc(ins, "sV")
            self.wait("s", tk_var)
            ins = s.activation(rb, rb, AF.Sqrt)
            tk_sd = self.inc(ins, "sS")
            self.wait("t", tk_var, tk_sd)
            self.bwait(0); self.bwait(1)
            self.mm(self.bank[0], self.onesf[:, :], rb, True, True)
            iv = self.mm(self.bank[1], self.onesf[:, :], ra, True, True)
            tk_mm = self.inc(iv, "sT")
            self.wait("v", tk_mm)
            v.reciprocal(self.urep[:, sl], self.bank[0])
            ins = v.scalar_tensor_tensor(self.vrep[:, sl], self.bank[1], -1.0,
                                         self.urep[:, sl],
                                         op0=ALU.mult, op1=ALU.mult)
            tk_ev = self.inc(ins, "sV")
            self.bank_free[0] = tk_ev
            self.bank_free[1] = tk_ev
            self._uv_tok = tk_ev

        def ln_dram(src_dram, aff, out_bf16, pre_apply=None):
            tk_t0 = self.tok("sT")
            self.wait("v", tk_t0); self.wait("s", tk_t0)
            sc_col, bi_col = aff
            tk_apply = []
            xw = self.xw
            pend_ap = None

            def do_applies(ta):
                for c in range(KC):
                    _lap(self.b16[:, c * t + ta * 512: c * t + ta * 512 + 512],
                         c, ta * 512, sc_col, bi_col, None, out_bf16)
                tk_apply.append((self.tok("sV"), self.tok("sS")))

            xslots = self.xtmp + self.xtmp2
            xfree = {i: self.xtmp_free[i % 2] for i in range(4)}
            for tt in range(self.n512):
                hh = (tt * 512) // xw
                for c in range(KC):
                    si = c % 4
                    slot = xslots[si]
                    if self._x_piece.get((hh, c)) is None:
                        self.wait("y", xfree[si])
                        h0 = self.dma("y", slot[0:64, 0:xw],
                                      src_dram(c)[0:64, hh * xw:(hh + 1) * xw],
                                      "sX0")
                        h1 = self.dma("y", slot[64:128, 0:xw],
                                      src_dram(c)[64:128, hh * xw:(hh + 1) * xw],
                                      "sX1")
                        self._x_piece[(hh, c)] = (h0, h1)
                    if (hh, c) not in self._x_cast:
                        h0, h1 = self._x_piece[(hh, c)]
                        self.wait("v", h0, h1); self.wait("s", h0, h1)
                        col = c * t + hh * xw
                        insv = v.tensor_copy(self.b16[:, col:col + xw],
                                             slot[:, 0:xw])
                        inss = s.activation(self.xsq16[:, col:col + xw],
                                            slot[:, 0:xw], AF.Square)
                        self.inc(inss, "sS")
                        xfree[si] = self.inc(insv, "sV")
                        self._x_cast.add((hh, c))
                tkv, tks = self.tok("sV"), self.tok("sS")
                self.bwait(4); self.bwait(5)
                self.wait("t", tkv, tks)
                for c in range(KC):
                    sl2 = slice(c * t + tt * 512, c * t + tt * 512 + 512)
                    self.mm(self.bslice(4, 0, 512)[0:1, :], self.ones128b[:, :],
                            self.b16[:, sl2], start=(c == 0), stop=(c == KC - 1))
                    ins = self.mm(self.bslice(5, 0, 512)[0:1, :], self.ones128b[:, :],
                                  self.xsq16[:, sl2], start=(c == 0),
                                  stop=(c == KC - 1))
                tk_mm = self.inc(ins, "sT")
                self.wait("v", tk_mm)
                v.tensor_copy(self.rowA[0:1, tt * 512:(tt + 1) * 512],
                              self.bslice(4, 0, 512)[0:1, :])
                ins = v.tensor_copy(self.rowB[0:1, tt * 512:(tt + 1) * 512],
                                    self.bslice(5, 0, 512)[0:1, :])
                tk_ev = self.inc(ins, "sV")
                self.bank_free[4] = tk_ev
                self.bank_free[5] = tk_ev
                _ln_tail_tt(tt)
                if pend_ap is not None:
                    if pre_apply is not None:
                        pre_apply()
                        pre_apply = None
                    do_applies(pend_ap)
                pend_ap = tt
            self._xtmp23_free = (xfree[2], xfree[3])
            self.xtmp_free = [xfree[0], xfree[1]]
            if pre_apply is not None:
                pre_apply()
            do_applies(pend_ap)
            return tk_apply

        def ln_sbuf(src_sbuf, aff, out_fp32, out_bf16, src_ready=None,
                    xsq=None, gp=False):
            if xsq is None:
                xsq = self.xsqT
            if src_ready is None:
                tk_t0 = self.tok("sT")
                tk_v0 = self.tok("sV")
                self.wait("v", tk_t0)
                self.wait("s", tk_t0, tk_v0)
                self.wait("t", tk_v0)
                if gp:
                    self.wait("g", tk_t0, tk_v0)
            tk_apply = []
            pend_ap = None
            sc_col, bi_col = aff

            def do_applies(ta):
                for c in range(KC):
                    _lap(src_sbuf(c)[:, ta * 512:(ta + 1) * 512], c, ta * 512,
                         sc_col, bi_col, out_fp32, out_bf16, gp=gp)
                tk_apply.append((self.tok("sV"), self.tok("sS")))

            for tt in range(self.n512):
                if src_ready is not None:
                    self.wait("v", *src_ready[tt])
                    self.wait("s", *src_ready[tt])
                    if gp:
                        self.wait("g", *src_ready[tt])
                for c in range(KC):
                    sl = src_sbuf(c)[:, tt * 512:(tt + 1) * 512]
                    ins = s.activation(xsq[:, c * 512:(c + 1) * 512], sl, AF.Square)
                tk_sq = self.inc(ins, "sS")
                self.bwait(4); self.bwait(5)
                self.wait("t", tk_sq)
                if src_ready is not None:
                    self.wait("t", *src_ready[tt])
                for c in range(KC):
                    sl = src_sbuf(c)[:, tt * 512:(tt + 1) * 512]
                    self.mm(self.bslice(4, 0, 512)[0:1, :], self.ones128f[:, :],
                            sl, start=(c == 0), stop=(c == KC - 1))
                    ins = self.mm(self.bslice(5, 0, 512)[0:1, :], self.ones128b[:, :],
                                  xsq[:, c * 512:(c + 1) * 512],
                                  start=(c == 0), stop=(c == KC - 1))
                tk_mm = self.inc(ins, "sT")
                self.wait("s", tk_mm)
                self.wait("v", tk_mm)
                v.tensor_copy(self.rowA[0:1, tt * 512:(tt + 1) * 512],
                              self.bslice(4, 0, 512)[0:1, :])
                ins = v.tensor_copy(self.rowB[0:1, tt * 512:(tt + 1) * 512],
                                    self.bslice(5, 0, 512)[0:1, :])
                tk_ev = self.inc(ins, "sV")
                self.bank_free[4] = tk_ev
                self.bank_free[5] = tk_ev
                _ln_tail_tt(tt)
                if pend_ap is not None:
                    do_applies(pend_ap)
                pend_ap = tt
            do_applies(pend_ap)
            return tk_apply

        def xsrc(c):
            return self.i_xT[c * 128:(c + 1) * 128, :]

        tk_h_apply = ln_dram(xsrc, (self.acol, self.sacol),
                lambda c, tt0: self.b16[:, c * t + tt0: c * t + tt0 + 512],
                pre_apply=emit_mod_derive)

        self.wait("y", *self._xtmp23_free)
        for half in range(2):
            sl = slice(half * 512, (half + 1) * 512)
            ta = self.dma("y", self.tmpA[0:1, 0:512], self.i_bkr[0:1, sl], "sX0")
            tb = self.dma("y", self.tmpB[0:1, 0:512], self.i_bvr[0:1, sl], "sX1")
            self.wait("t", ta, tb, tk_consts)
            self.bwait(4); self.bwait(5)
            self.mm(self.bank[4], self.onesf[:, :], self.tmpA[0:1, 0:512], True, True)
            i1 = self.mm(self.bank[5], self.onesf[:, :], self.tmpB[0:1, 0:512],
                         True, True)
            tk_bc = self.inc(i1, "sT")
            self.wait("v", tk_bc)
            v.tensor_copy(self.bkrep[:, sl], self.bank[4])
            ins = v.tensor_copy(self.bvrep[:, sl], self.bank[5])
            tk_rep = self.inc(ins, "sV")
            self.bank_free[4] = tk_rep
            self.bank_free[5] = tk_rep
            self.wait("y", tk_rep)
        rowrb = self.dma("y", self.tmpA[0:1, 0:NE], self.i_rbr, "sX0")
        self.wait("t", rowrb)
        self.bwait(4)
        ins = self.mm(self.bank[4][:, 0:NE], self.onesf[:, :],
                      self.tmpA[0:1, 0:NE], True, True)
        tk_rb = self.inc(ins, "sT")
        self.wait("v", tk_rb)
        ins = v.tensor_copy(self.rbrep[:, :], self.bank[4][:, 0:NE])
        tk_rbv = self.inc(ins, "sV")
        self.bank_free[4] = tk_rbv
        tk_h16 = self.tok("sV")
        self.dump("modcol", self.modcol[:, :])
        self.dump("h16", self.b16[:, :])
        self.dump("urep0", self.urep[:, :])
        self.dump("vrep0", self.vrep[:, :])

        self.wait("g", self.xtmp_free[0], self.xtmp_free[1], self._xtmp_vtok)
        wtk["wq"] = [self.wdma(self.wq[:, :],
                               self.i_Wq.rearrange("(k p) j -> p k j", p=128))]
        ins = v.memset(self.kvacc[:, :], 0.0)
        self.inc(ins, "sV")
        self.wait("t", *wtk["wk"], *wtk["wv"])
        kva = self.kvacc.ap().rearrange("p (c m) -> p c m", m=65)
        kv_done = [None, None]
        tk_kvacc = None

        def emit_kv(pk):
            nonlocal tk_kvacc
            r_, tk_k16_, tk_vsb_ = pk
            self.wait("t", tk_k16_, tk_vsb_)
            self.bwait(0); self.bwait(1)
            for h in range(H):
                pair = h // 2
                b = pair // 4
                col0 = (pair % 4) * 128
                odd = h % 2
                outap = self.ps[odd * 64:(odd + 1) * 64,
                                b * 512 + col0: b * 512 + col0 + 65]
                ins = self.mm(outap, self.k16r[r_][:, h * 64:(h + 1) * 64],
                              self.vsbr[r_][:, h * 65:(h + 1) * 65], True, True,
                              tile_position=(0, odd * 64))
            tk_kv = self.inc(ins, "sT")
            kv_done[r_] = tk_kv
            self.wait("v", tk_kv)
            for bi in range(2):
                psa = self.ps[:, bi * 512: bi * 512 + 512].rearrange(
                    "p (c m) -> p c m", m=128)
                ins = v.tensor_tensor(kva[:, bi * 4:(bi + 1) * 4, :],
                                      kva[:, bi * 4:(bi + 1) * 4, :],
                                      psa[:, :, 0:65], op=ALU.add)
            tk_kvacc = self.inc(ins, "sV")
            self.bank_free[0] = tk_kvacc
            self.bank_free[1] = tk_kvacc

        pend_kv = None
        for i in range(self.n128):
            self.wait("t", *tk_h_apply[(i * 128) // 512])
            r = i % 2
            vs3 = vs3r[r]
            for jh in range(2):
                b = (2, 3, 6)[(i * 2 + jh) % 3]
                self.bwait(b)
                for k in range(KC):
                    ins = self.mm(self.bank[b],
                                  self.b16[:, k * t + i * 128: k * t + i * 128 + 128],
                                  self.wk[:, k * KB + jh * 512: k * KB + (jh + 1) * 512],
                                  start=(k == 0), stop=(k == KC - 1))
                tk_k = self.inc(ins, "sT")
                self.wait("v", tk_k, kv_done[r])
                v.tensor_tensor(self.ketmp[:, jh * 512:(jh + 1) * 512], self.bank[b],
                                self.bkrep[:, jh * 512:(jh + 1) * 512], op=ALU.add)
                ins = v.tensor_scalar(self.tmpA[:, :],
                                      self.ketmp[:, jh * 512:(jh + 1) * 512],
                                      0.0, None, op0=ALU.min)
                tk_t1 = self.inc(ins, "sV")
                self.bank_free[b] = tk_t1
                self.wait("s", tk_t1)
                ins = s.activation(self.tmpB[:, :], self.tmpA[:, :], AF.Exp)
                tk_t2 = self.inc(ins, "sS")
                self.wait("v", tk_t2)
                ins = v.scalar_tensor_tensor(self.k16r[r][:, jh * 512:(jh + 1) * 512],
                                             self.ketmp[:, jh * 512:(jh + 1) * 512],
                                             0.0, self.tmpB[:, :],
                                             op0=ALU.max, op1=ALU.add)
                tk_k16 = self.inc(ins, "sV")
            for jh in range(2):
                b = 4 + jh
                self.bwait(b)
                for k in range(KC):
                    ins = self.mm(self.bank[b],
                                  self.b16[:, k * t + i * 128: k * t + i * 128 + 128],
                                  self.wv[:, k * KB + jh * 512: k * KB + (jh + 1) * 512],
                                  start=(k == 0), stop=(k == KC - 1))
                tk_v = self.inc(ins, "sT")
                self.wait("v", tk_v)
                ins = v.tensor_tensor(vs3[:, jh * 8:(jh + 1) * 8, 0:64], self.bank[b],
                                      self.bvrep[:, jh * 512:(jh + 1) * 512],
                                      op=ALU.add)
                tk_vsb = self.inc(ins, "sV")
                self.bank_free[b] = tk_vsb
            if pend_kv is not None:
                emit_kv(pend_kv)
            pend_kv = (r, tk_k16, tk_vsb)
        emit_kv(pend_kv)

        self.dump("kvacc", self.kvacc[:, :])
        self.wait("y", tk_kvacc)
        tk_ccin = self.dma("y", self.cc_in.ap(), self.kvacc[:, :], "sX0")
        self.wait("g", tk_ccin)
        ins = g.collective_compute(
            "AllReduce", ALU.add,
            replica_groups=[list(range(self.ncores))],
            ins=[self.cc_in.ap().opt()], outs=[self.cc_out.ap().opt()])
        tk_cc = self.inc(ins, "sCC")

        self.wait("g", self.tok("sV"))
        tk_wo_a = [self.wdma(self.wo_a[:, :],
                             self.i_Wo.rearrange("(k p) j -> p k j", p=128)[:, :, 0:512])]
        self.wait("t", *wtk["wq"])
        qp_hist = {0: None, 1: None}
        for jc in range(KC):
            for tt in range(self.n512):
                u_i = jc * self.n512 + tt
                pp = u_i % 2
                rl_buf = (self.tmpC, self.tmpD)[pp]
                ex_buf = (self.tmpB, self.tmpE)[pp]
                b = 2 + u_i % 4
                self.bwait(b)
                for k in range(KC):
                    ins = self.mm(self.bank[b],
                                  self.wq[:, k * KB + jc * 128: k * KB + jc * 128 + 128],
                                  self.b16[:, k * t + tt * 512: k * t + (tt + 1) * 512],
                                  start=(k == 0), stop=(k == KC - 1))
                tk_q = self.inc(ins, "sT")
                self.wait("v", tk_q)
                ins = v.tensor_scalar(self.tmpA[:, :], self.bank[b],
                                      self.bqc[:, jc:jc + 1], None, op0=ALU.add)
                tk_qc = self.inc(ins, "sV")
                self.bank_free[b] = tk_qc
                self.wait("s", tk_qc, qp_hist[pp])
                ins = s.activation(rl_buf[:, :], self.tmpA[:, :], AF.Relu)
                tk_rl = self.inc(ins, "sS")
                self.wait("v", tk_rl)
                ins = v.tensor_tensor(self.tmpA[:, :], self.tmpA[:, :],
                                      rl_buf[:, :], op=ALU.subtract)
                tk_mn = self.inc(ins, "sV")
                self.wait("s", tk_mn)
                ins = s.activation(ex_buf[:, :], self.tmpA[:, :], AF.Exp)
                tk_qe = self.inc(ins, "sS")
                self.wait("v", tk_qe)
                ins = v.tensor_tensor(
                    self.qp[:, jc * t + tt * 512: jc * t + (tt + 1) * 512],
                    rl_buf[:, :], ex_buf[:, :], op=ALU.add)
                tk_qp = self.inc(ins, "sV")
                qp_hist[pp] = tk_qp

        self.wait("g", tk_cc)
        tk_kvg = self.wdma(self.kvg[:, :], self.cc_out.ap())
        self.wait("v", tk_kvg)
        ins = v.tensor_copy(self.kvg16[:, :], self.kvg[:, :])
        self.inc(ins, "sV")
        v.memset(self.ksdf[:, :], 0.0)
        for h in range(H):
            k = h // 2
            odd = h % 2
            ins = v.tensor_copy(
                self.ksdf[odd * 64:(odd + 1) * 64, k * 16 + h:k * 16 + h + 1],
                self.kvg[odd * 64:(odd + 1) * 64, k * 65 + 64: k * 65 + 65])
        self.vfence(ins)
        ins = v.tensor_copy(self.ksd16[:, :], self.ksdf[:, :])
        tk_ksd = self.inc(ins, "sV")

        self.dump("qp", self.qp[:, :])
        self.dump("kvg", self.kvg[:, :])
        self.dump("ksdf", self.ksdf[:, :])
        self.wait("g", self.tok("sT"), self.tok("sV"))
        tk_wo_grp = [self.wdma(
            self.wo_b[:, :],
            self.i_Wo.rearrange("(k p) j -> p k j", p=128)[:, :, 512:1024])] + tk_wo_a

        def wo_lhs(k, jc):
            if jc < 4:
                return self.wo_a[:, k * 512 + jc * 128: k * 512 + jc * 128 + 128]
            return self.wo_b[:, k * 512 + (jc - 4) * 128: k * 512 + (jc - 4) * 128 + 128]

        self.wait("t", tk_ksd, tk_qp)
        tk_c1_tt = []
        tk_z_r = [None, None]

        def emit_den(tt):
            self.bwait(4)
            for k in range(KC):
                ins = self.mm(self.bslice(4, 0, 512)[0:16, :],
                              self.ksd16[:, k * 16:(k + 1) * 16],
                              self.qp[:, k * t + tt * 512: k * t + (tt + 1) * 512],
                              start=(k == 0), stop=(k == KC - 1))
            tk_den = self.inc(ins, "sT")
            self.wait("v", tk_den)
            if tk_z_r[tt % 2] is not None:
                self.wait("v", self._z_read[tt % 2])
            ins = v.tensor_scalar(self.den[:, :], self.bslice(4, 0, 512)[0:16, :],
                                  EPS_ATTN, None, op0=ALU.add)
            tk_dv = self.inc(ins, "sV")
            self.bank_free[4] = tk_dv
            v.reciprocal(self.zrow[:, :], self.den[:, :])
            ins = v.tensor_copy(self.z16r[tt % 2][:, :], self.zrow[:, :])
            tk_z_r[tt % 2] = self.inc(ins, "sV")

        self._z_read = [None, None]
        emit_den(0)
        for tt in range(self.n512):
            if tt + 1 < self.n512:
                emit_den(tt + 1)
            z16 = self.z16r[tt % 2]
            tk_z = tk_z_r[tt % 2]
            self.wait("t", tk_z)
            for c in range(KC):
                bz = 5 if c % 2 == 0 else 2
                self.bwait(bz)
                ins = self.mm(self.bank[bz], self.esel[:, c * 128:(c + 1) * 128],
                              z16[:, :], True, True)
                tk_zr = self.inc(ins, "sT")
                self.wait("v", tk_zr)
                ins = v.tensor_copy(self.zrep[:, (c % 2) * 0:512], self.bank[bz])
                tk_zrv = self.inc(ins, "sV")
                self.bank_free[bz] = tk_zrv
                self.bwait(6)
                self.mm(self.ps[0:64, 6 * 512:7 * 512],
                        self.kvg16[0:64, c * 65: c * 65 + 64],
                        self.qp[0:64, c * t + tt * 512: c * t + (tt + 1) * 512],
                        True, True)
                i1 = self.mm(self.ps[64:128, 6 * 512:7 * 512],
                             self.kvg16[64:128, c * 65: c * 65 + 64],
                             self.qp[64:128, c * t + tt * 512: c * t + (tt + 1) * 512],
                             True, True, tile_position=(64, 64))
                tk_num = self.inc(i1, "sT")
                self.wait("v", tk_num, tk_zrv)
                ins = v.tensor_tensor(self.attn16[:, c * 512:(c + 1) * 512],
                                      self.bank[6], self.zrep[:, :], op=ALU.mult)
                tk_at = self.inc(ins, "sV")
                self.bank_free[6] = tk_at
            self._z_read[tt % 2] = self.tok("sT")
            self.wait("t", tk_at, *tk_wo_grp)
            for jc in range(KC):
                b = (2, 3, 0, 1)[jc % 4]
                self.bwait(b)
                for k in range(KC):
                    self.mm(self.bank[b], wo_lhs(k, jc),
                            self.attn16[:, k * 512:(k + 1) * 512],
                            start=(k == 0), stop=False)
                ins = self.mm(self.bank[b], self.id128[:, :],
                              self.b16[:, jc * t + tt * 512: jc * t + (tt + 1) * 512],
                              start=False, stop=True)
                tk_womm = self.inc(ins, "sT")
                self.wait("v", tk_womm)
                ins = v.tensor_scalar(
                    self.bufa[:, jc * t + tt * 512: jc * t + (tt + 1) * 512],
                    self.bank[b], self.boc[:, jc:jc + 1], None, op0=ALU.add)
                tk_bo = self.inc(ins, "sV")
                self.bank_free[b] = tk_bo
            tk_c1_tt.append((tk_bo, tk_bo))

        self.dump("proj", self.bufa[:, :])
        self.dump("attn16_last", self.attn16[:, :])
        self.dump("zrow_last", self.zrow[:, :])
        ffn_tok = {}

        def load_ffn_eighth(qq, slot_w1, slot_w2):
            self.wgrp_start()
            self.wdma(self.ffnslot[slot_w1][:, :],
                      self.i_W1.rearrange("(k p) j -> p k j", p=128)[
                          :, :, qq * 512:(qq + 1) * 512])
            self.wdma(self.ffnslot[slot_w2][:, :],
                      self.i_W2[qq * 512:(qq + 1) * 512, :].rearrange(
                          "(k p) j -> p k j", p=128))
            ffn_tok[qq] = self.wgrp_end()

        self.wait("g", self.tok("sT"))
        load_ffn_eighth(0, 0, 1)
        load_ffn_eighth(1, 2, 3)

        def bufa_c(c):
            return self.bufa[:, c * t:(c + 1) * t]

        tk_apply1 = ln_sbuf(bufa_c, (self.g1c, self.be1c),
                lambda c, tt0: self.bufa[:, c * t + tt0: c * t + tt0 + 512],
                lambda c, tt0: self.b16[:, c * t + tt0: c * t + tt0 + 512],
                src_ready=tk_c1_tt, xsq=self.g16)

        self.dump("a_ln", self.bufa[:, :])
        self.dump("a_ln16", self.b16[:, :])
        self.xtmp_free = [self.tok("sT"), self.tok("sT")]
        self._xtmp_vtok = self.tok("sT")

        nq = 8
        tk_ffn_tt = []
        for qq in range(nq):
            s_w1 = (qq % 2) * 2
            s_w2 = s_w1 + 1
            self.wait("t", *ffn_tok[qq])
            for tt in range(self.n512):
                self.wait("t", *tk_apply1[tt])
                for jj in range(4):
                    b = 2 + jj % 2
                    self.bwait(b)
                    for k in range(KC):
                        ins = self.mm(
                            self.bank[b],
                            self.ffnslot[s_w1][:, k * 512 + jj * 128: k * 512 + jj * 128 + 128],
                            self.b16[:, k * t + tt * 512: k * t + (tt + 1) * 512],
                            start=(k == 0), stop=(k == KC - 1))
                    tk_w1 = self.inc(ins, "sT")
                    self.wait("s", tk_w1)
                    ins = s.activation(self.relu16[:, jj * 512:(jj + 1) * 512],
                                       self.bank[b], AF.Relu,
                                       bias=self.b1c[:, qq * 4 + jj: qq * 4 + jj + 1])
                    tk_r = self.inc(ins, "sS")
                    self.bank_free[b] = tk_r
                self.wait("t", tk_r)
                for jc in range(KC):
                    b = (5, 6, 0, 1)[jc % 4]
                    self.bwait(b)
                    for kk in range(4):
                        ins = self.mm(
                            self.bank[b],
                            self.ffnslot[s_w2][:, kk * KB + jc * 128: kk * KB + jc * 128 + 128],
                            self.relu16[:, kk * 512:(kk + 1) * 512],
                            start=(kk == 0), stop=(kk == 3))
                    tk_w2 = self.inc(ins, "sT")
                    self.wait("v", tk_w2)
                    dst = self.bufa[:, jc * t + tt * 512: jc * t + (tt + 1) * 512]
                    if qq == 0:
                        ins = v.scalar_tensor_tensor(dst, self.bank[b],
                                                     self.b2c[:, jc:jc + 1], dst,
                                                     op0=ALU.add, op1=ALU.add)
                    else:
                        ins = v.tensor_tensor(dst, self.bank[b], dst, op=ALU.add)
                    tk_y2 = self.inc(ins, "sV")
                    self.bank_free[b] = tk_y2
                if qq == nq - 1:
                    tk_ffn_tt.append((self.tok("sV"), self.tok("sV")))
            if qq + 2 < nq:
                self.wait("g", self.tok("sT"))
                load_ffn_eighth(qq + 2, s_w1, s_w2)

        def load_mat_16k(dst2, srcw):
            self.wgrp_start()
            self.wdma(self.exslot[dst2[0]][:, :],
                      srcw[0:512, :].rearrange("(k p) j -> p k j", p=128))
            self.wdma(self.exslot[dst2[1]][:, :],
                      srcw[512:1024, :].rearrange("(k p) j -> p k j", p=128))
            return self.wgrp_end()

        self.dump("s_ffn", self.bufa[:, :])
        self.wait("g", self.tok("sT"), self.tok("sS"))
        tk_shg = load_mat_16k((0, 1), self.i_shg)
        tk_shu = load_mat_16k((2, 3), self.i_shu)
        xw = self.xw
        x_tok = {}

        def load_x_piece(pidx):
            hh, c = pidx // KC, pidx % KC
            slot = self.xtmp[pidx % 2]
            self.wait("y", self.xtmp_free[pidx % 2])
            h0 = self.dma("y", slot[0:64, 0:xw],
                          xsrc(c)[0:64, hh * xw:(hh + 1) * xw], "sX0")
            h1 = self.dma("y", slot[64:128, 0:xw],
                          xsrc(c)[64:128, hh * xw:(hh + 1) * xw], "sX1")
            x_tok[pidx] = (h0, h1)

        load_x_piece(0)
        load_x_piece(1)
        for tt in range(self.n512):
            self.wait("v", *tk_ffn_tt[tt])
            self.wait("s", *tk_ffn_tt[tt])
            for c in range(KC):
                sl = bufa_c(c)[:, tt * 512:(tt + 1) * 512]
                ins = s.activation(self.g16[:, c * 512:(c + 1) * 512], sl,
                                   AF.Square)
            tk_sq = self.inc(ins, "sS")
            self.bwait(4); self.bwait(5)
            self.wait("t", tk_sq, *tk_ffn_tt[tt])
            for c in range(KC):
                sl = bufa_c(c)[:, tt * 512:(tt + 1) * 512]
                self.mm(self.bslice(4, 0, 512)[0:1, :], self.ones128f[:, :],
                        sl, start=(c == 0), stop=(c == KC - 1))
                ins = self.mm(self.bslice(5, 0, 512)[0:1, :], self.ones128b[:, :],
                              self.g16[:, c * 512:(c + 1) * 512],
                              start=(c == 0), stop=(c == KC - 1))
            tk_mm = self.inc(ins, "sT")
            self.wait("s", tk_mm)
            self.wait("v", tk_mm)
            v.tensor_copy(self.rowA[0:1, tt * 512:(tt + 1) * 512],
                          self.bslice(4, 0, 512)[0:1, :])
            ins = v.tensor_copy(self.rowB[0:1, tt * 512:(tt + 1) * 512],
                                self.bslice(5, 0, 512)[0:1, :])
            tk_ev = self.inc(ins, "sV")
            self.bank_free[4] = tk_ev
            self.bank_free[5] = tk_ev
            _ln_tail_tt(tt)
        tk_ln2e = [None] * self.n512
        for hh in range(self.nxh):
            for c in range(KC):
                pidx = hh * KC + c
                if pidx not in x_tok:
                    load_x_piece(pidx)
                slot = self.xtmp[pidx % 2]
                self.wait("v", *x_tok[pidx])
                pp = (self.lntB, self.combSB)[c % 2]
                for q in range(xw // 512):
                    tt0 = hh * xw + q * 512
                    src = self.bufa[:, c * t + tt0: c * t + tt0 + 512]
                    v.scalar_tensor_tensor(pp[:, :], src,
                                           self.g2ga[:, c:c + 1],
                                           self.urep[:, tt0:tt0 + 512],
                                           op0=ALU.mult, op1=ALU.mult)
                    v.scalar_tensor_tensor(pp[:, :],
                                           self.vrep[:, tt0:tt0 + 512],
                                           self.g2ga[:, c:c + 1], pp[:, :],
                                           op0=ALU.mult, op1=ALU.add)
                    ins = v.scalar_tensor_tensor(src, pp[:, :],
                                                 self.be2ga[:, c:c + 1],
                                                 slot[:, q * 512:(q + 1) * 512],
                                                 op0=ALU.add, op1=ALU.add)
                tk_res = self.inc(ins, "sV")
                self.xtmp_free[pidx % 2] = tk_res
                if pidx + 2 < self.nxh * KC:
                    load_x_piece(pidx + 2)
            tk_hh = (self.tok("sV"), self.tok("sV"))
            for q in range(xw // 512):
                tk_ln2e[hh * (xw // 512) + q] = tk_hh
        self.wait("g", self.tok("sV"))
        tk_shd = load_mat_16k((4, 5), self.i_shd)

        tk_apply2 = ln_sbuf(bufa_c, (self.bcol, self.sbcol), None,
                lambda c, tt0: self.b16[:, c * t + tt0: c * t + tt0 + 512],
                src_ready=tk_ln2e, xsq=self.g16)
        tk_h2 = self.tok("sV")

        def sh_lhs(slotpair, k, jcol):
            sl = self.exslot[slotpair[0] if k < 4 else slotpair[1]]
            kk = k % 4
            return sl[:, kk * KB + jcol: kk * KB + jcol + 128]
        shd_lhs = lambda k, jcol: sh_lhs((4, 5), k, jcol)

        self.wait("t", *tk_shg, *tk_shu, *tk_shd)
        self.wait("v", self.tok("sS"))
        prev_shh = None
        for tt in range(self.n512):
            self.wait("t", *tk_apply2[tt])
            rhs_h2 = [self.b16[:, k * t + tt * 512: k * t + (tt + 1) * 512]
                      for k in range(KC)]
            self.wait("s", *tk_apply2[tt])
            for hc in range(KC):
                bg = (4, 0)[hc % 2]
                bu = (5, 1)[hc % 2]
                self.bwait(bg); self.bwait(bu)
                for k in range(KC):
                    ins = self.mm(self.bank[bg], sh_lhs((0, 1), k, hc * 128),
                                  rhs_h2[k], start=(k == 0), stop=(k == KC - 1))
                tk_g8 = self.inc(ins, "sT")
                for k in range(KC):
                    ins = self.mm(self.bank[bu], sh_lhs((2, 3), k, hc * 128),
                                  rhs_h2[k], start=(k == 0), stop=(k == KC - 1))
                tk_gu = self.inc(ins, "sT")
                self.wait("s", tk_g8)
                if prev_shh is not None:
                    self.wait("s", prev_shh)
                ins = s.activation(self.g16[:, hc * 512:(hc + 1) * 512],
                                   self.bank[bg], AF.Silu)
                tk_sl = self.inc(ins, "sS")
                self.bank_free[bg] = tk_sl
                self.wait("v", tk_sl, tk_gu)
                if prev_shh is not None:
                    self.wait("v", prev_shh)
                ins = v.tensor_tensor(self.shh16[:, hc * 512:(hc + 1) * 512],
                                      self.bank[bu],
                                      self.g16[:, hc * 512:(hc + 1) * 512],
                                      op=ALU.mult)
                tk_shh = self.inc(ins, "sV")
                self.bank_free[bu] = tk_shh
            self.wait("t", tk_shh)
            for jc in range(KC):
                b = (2, 3, 6)[jc % 3]
                self.bwait(b)
                for k in range(KC):
                    ins = self.mm(self.bank[b], shd_lhs(k, jc * 128),
                                  self.shh16[:, k * 512:(k + 1) * 512],
                                  start=(k == 0), stop=(k == KC - 1))
                tk_sd = self.inc(ins, "sT")
                self.wait("v", tk_sd)
                dst = self.bufa[:, jc * t + tt * 512: jc * t + (tt + 1) * 512]
                ins = v.scalar_tensor_tensor(dst, self.bank[b],
                                             self.gbcol[:, jc:jc + 1], dst,
                                             op0=ALU.mult, op1=ALU.add)
                tk_se = self.inc(ins, "sV")
                self.bank_free[b] = tk_se
            prev_shh = ("sT", self.sems["sT"][1])

        exw = {}

        def load_expert(e, s_up, s_gt, s_dn, wait_tok):
            if wait_tok is not None:
                self.wait("g", wait_tok)
            self.wgrp_start()
            self.wdma(self.exslot[s_up][:, :],
                      self.i_up[e, :, :].rearrange("(k p) j -> p k j", p=128))
            self.wdma(self.exslot[s_gt][:, :],
                      self.i_gp[e, :, :].rearrange("(k p) j -> p k j", p=128))
            self.wdma(self.exslot[s_dn][:, :],
                      self.i_dp[e, :, :].rearrange("(k p) j -> p k j", p=128))
            exw[e] = (s_up, s_gt, s_dn, self.wgrp_end())

        nt = self.n128
        self.wait("t", tk_rwt)
        for i in range(nt):
            self.wait("t", *tk_apply2[(i * 128) // 512])
            b = (4, 5, 2, 3)[i % 4]
            self.bwait(b)
            for k in range(KC):
                ins = self.mm(self.bslice(b, 0, NE),
                              self.b16[:, k * t + i * 128: k * t + i * 128 + 128],
                              self.rwT[:, k * NE:(k + 1) * NE],
                              start=(k == 0), stop=(k == KC - 1))
            tk_lg = self.inc(ins, "sT")
            self.wait("v", tk_lg)
            ins = v.tensor_tensor(self.ltm[:, i * NE:(i + 1) * NE],
                                  self.bslice(b, 0, NE), self.rbrep[:, :],
                                  op=ALU.add)
            tk_lv = self.inc(ins, "sV")
            self.bank_free[b] = tk_lv
        lt3 = self.ltm.ap().rearrange("p (i e) -> p i e", e=NE)
        self.vfence(v.reduce_max(self.mx1[:, :], lt3, axis=AX))
        for i in range(nt):
            v.tensor_scalar(self.msk1[:, i * NE:(i + 1) * NE],
                            self.ltm[:, i * NE:(i + 1) * NE],
                            self.mx1[:, i:i + 1], None, op0=ALU.is_equal)
        for i in range(nt):
            v.scalar_tensor_tensor(self.msk2[:, i * NE:(i + 1) * NE],
                                   self.msk1[:, i * NE:(i + 1) * NE], -1e30,
                                   self.ltm[:, i * NE:(i + 1) * NE],
                                   op0=ALU.mult, op1=ALU.add)
        m23 = self.msk2.ap().rearrange("p (i e) -> p i e", e=NE)
        self.vfence(v.reduce_max(self.mx2[:, :], m23, axis=AX))
        for i in range(nt):
            v.tensor_scalar(self.msk2[:, i * NE:(i + 1) * NE],
                            self.msk2[:, i * NE:(i + 1) * NE],
                            self.mx2[:, i:i + 1], None, op0=ALU.is_equal)
        ins = v.tensor_tensor(self.s1a[:, :], self.mx1[:, :], self.mx2[:, :],
                              op=ALU.subtract)
        tk_dd = self.inc(ins, "sV")
        self.wait("s", tk_dd)
        ins = s.activation(self.s1a[:, :], self.s1a[:, :], AF.Sigmoid)
        tk_sg = self.inc(ins, "sS")
        self.wait("v", tk_sg)
        v.tensor_scalar(self.s2a[:, :], self.s1a[:, :], -1.0, 1.0,
                        op0=ALU.mult, op1=ALU.add)
        for i in range(nt):
            v.tensor_scalar(self.msk1[:, i * NE:(i + 1) * NE],
                            self.msk1[:, i * NE:(i + 1) * NE],
                            self.s1a[:, i:i + 1], None, op0=ALU.mult)
        for i in range(nt):
            v.scalar_tensor_tensor(self.msk2[:, i * NE:(i + 1) * NE],
                                   self.msk2[:, i * NE:(i + 1) * NE],
                                   self.s2a[:, i:i + 1],
                                   self.msk1[:, i * NE:(i + 1) * NE],
                                   op0=ALU.mult, op1=ALU.add)
        ins = v.tensor_copy(self.comb16[:, :], self.msk2[:, :])
        tk_cmb = self.inc(ins, "sV")
        tk_tp_ev = None
        self.wait("t", tk_cmb)
        nw = (nt + 7) // 8
        for w in range(nw):
            if tk_tp_ev is not None:
                self.wait("t", tk_tp_ev)
            lo = w * 8
            hi = min(nt, lo + 8)
            for i in range(lo, hi):
                ins = te.transpose(self.pstb[0:16, (i - lo) * 128:(i - lo + 1) * 128],
                                   self.comb16[:, i * NE:(i + 1) * NE],
                                   self.id128[:, :])
            tk_tp = self.inc(ins, "sT")
            self.wait("v", tk_tp)
            ins = v.tensor_copy(self.combine_fm[:, lo * 128: hi * 128],
                                self.pstb[0:16, 0:(hi - lo) * 128])
            tk_tp_ev = self.inc(ins, "sV")

        self.dump("h2", self.b16[:, :])
        self.dump("ltm", self.ltm[:, :])
        self.dump("combine_fm", self.combine_fm[:, :])
        tk_post_shared = self.tok("sT")
        self.wait("g", self.tok("sT"), self.tok("sV"))
        load_expert(0, 0, 1, 2, None)
        load_expert(1, 3, 4, 5, None)
        prev_hid = {0: tk_post_shared, 1: tk_post_shared}
        hsel = 0
        comb_bufs = [self.comb_all, self.comb_allB]
        comb_tok = {}

        def emit_comb_unit(e, tt):
            dstc = comb_bufs[e % 2]
            self.wait("t", tk_tp_ev)
            self.bwait(4)
            ins = self.mm(self.bank[4], self.selbank[:, e * 128:(e + 1) * 128],
                          self.combine_fm[:, tt * 512:(tt + 1) * 512],
                          True, True)
            tk_cr = self.inc(ins, "sT")
            self.wait("v", tk_cr)
            ins = v.tensor_copy(dstc[:, tt * 512:(tt + 1) * 512], self.bank[4])
            tk_cb = self.inc(ins, "sV")
            self.bank_free[4] = tk_cb
            if tt == self.n512 - 1:
                comb_tok[e] = tk_cb

        def emit_comb(e):
            for tt in range(self.n512):
                emit_comb_unit(e, tt)

        emit_comb(0)

        def emit_downs(pd):
            e_, tt_, hid_, s_dn_, tk_hid_ = pd
            for jc in range(KC):
                b = 2 + jc % 2
                self.bwait(b)
                for kk in range(4):
                    self.wait("t", tk_hid_[kk])
                    ins = self.mm(self.bank[b],
                                  self.exslot[s_dn_][:, kk * KB + jc * 128: kk * KB + jc * 128 + 128],
                                  hid_[:, kk * 512:(kk + 1) * 512],
                                  start=(kk == 0), stop=(kk == 3))
                tk_dn = self.inc(ins, "sT")
                self.wait("v", tk_dn)
                dst = self.bufa[:, jc * t + tt_ * 512: jc * t + (tt_ + 1) * 512]
                ins = v.scalar_tensor_tensor(dst, self.bank[b],
                                             self.gbcol[:, jc:jc + 1], dst,
                                             op0=ALU.mult, op1=ALU.add)
                tk_de = self.inc(ins, "sV")
                self.bank_free[b] = tk_de
                if e_ == NE - 1:
                    self.wait("y", tk_de)
                    self.dma("y",
                             self.o_y[jc * 128:(jc + 1) * 128,
                                      tt_ * 512:(tt_ + 1) * 512],
                             self.bufa[:, jc * t + tt_ * 512:
                                       jc * t + (tt_ + 1) * 512],
                             ("sOut", "sOut1", "sOut2")[jc % 3])

        pending = None
        for e in range(NE):
            s_up, s_gt, s_dn, tk_w = exw[e]
            self.wait("t", *tk_w)
            comb = comb_bufs[e % 2]
            tk_comb = comb_tok[e]
            for tt in range(self.n512):
                rhs_h2 = [self.b16[:, k * t + tt * 512: k * t + (tt + 1) * 512]
                          for k in range(KC)]
                hid = self.hid16r[hsel]
                tk_hid = [None] * 4
                for ec in range(4):
                    bg = (5, 0)[ec % 2]
                    bu = (6, 1)[ec % 2]
                    self.bwait(bg); self.bwait(bu)
                    for k in range(KC):
                        ins = self.mm(self.bank[bg],
                                      self.exslot[s_gt][:, k * 512 + ec * 128: k * 512 + ec * 128 + 128],
                                      rhs_h2[k], start=(k == 0), stop=(k == KC - 1))
                    tk_g8 = self.inc(ins, "sT")
                    for k in range(KC):
                        ins = self.mm(self.bank[bu],
                                      self.exslot[s_up][:, k * 512 + ec * 128: k * 512 + ec * 128 + 128],
                                      rhs_h2[k], start=(k == 0), stop=(k == KC - 1))
                    tk_gu = self.inc(ins, "sT")
                    if tt == 0 and e + 1 < NE and self.n512 == 4:
                        emit_comb_unit(e + 1, ec)
                    self.wait("s", tk_g8)
                    if prev_hid.get(hsel) is not None:
                        self.wait("s", prev_hid[hsel])
                    ins = s.activation(self.g16[:, ec * 512:(ec + 1) * 512],
                                       self.bank[bg], AF.Silu)
                    tk_sl = self.inc(ins, "sS")
                    self.bank_free[bg] = tk_sl
                    self.wait("v", tk_gu, tk_comb)
                    if prev_hid.get(hsel) is not None:
                        self.wait("v", prev_hid[hsel])
                    ins = v.tensor_tensor(hid[:, ec * 512:(ec + 1) * 512],
                                          self.bank[bu],
                                          comb[:, tt * 512:(tt + 1) * 512],
                                          op=ALU.mult)
                    self.inc(ins, "sV")
                    self.bank_free[bu] = self.tok("sV")
                    self.wait("v", tk_sl)
                    ins = v.tensor_tensor(hid[:, ec * 512:(ec + 1) * 512],
                                          hid[:, ec * 512:(ec + 1) * 512],
                                          self.g16[:, ec * 512:(ec + 1) * 512],
                                          op=ALU.mult)
                    tk_hid[ec] = self.inc(ins, "sV")
                if pending is not None:
                    pe = pending[0]
                    emit_downs(pending)
                    prev_hid[hsel ^ 1] = self.tok("sT")
                    if pe != e and pe + 2 < NE:
                        load_expert(pe + 2, *exw[pe][:3], self.tok("sT"))
                pending = (e, tt, hid, s_dn, tk_hid)
                hsel ^= 1
                if tt == 0 and e + 1 < NE and self.n512 != 4:
                    emit_comb(e + 1)
        emit_downs(pending)

        self.wait("y", self.tok("sOut"), self.tok("sOut1"),
                  self.tok("sOut2"))
        return nc


def _col(vec, nchunk):
    return np.ascontiguousarray(np.asarray(vec, np.float32).reshape(nchunk, 128).T)


def _prep_shared(inp):
    d = {}
    d["Wq"] = np.ascontiguousarray(inp["L0_attn_Wq"], np.float32)
    d["Wk"] = np.ascontiguousarray(inp["L0_attn_Wk"], np.float32)
    d["Wv"] = np.ascontiguousarray(inp["L0_attn_Wv"], np.float32)
    d["Wo"] = np.ascontiguousarray(inp["L0_attn_Wo"], np.float32)
    d["W1"] = np.ascontiguousarray(inp["L0_attn_W1"], np.float32)
    d["W2"] = np.ascontiguousarray(inp["L0_attn_W2"], np.float32)
    d["bq_col"] = _col(inp["L0_attn_bq"], KC)
    d["bo_col"] = _col(inp["L0_attn_bo"], KC)
    d["b1_col"] = _col(inp["L0_attn_b1"], 32)
    d["b2_col"] = _col(inp["L0_attn_b2"], KC)
    d["bk_row"] = np.ascontiguousarray(
        np.asarray(inp["L0_attn_bk"], np.float32).reshape(1, D))
    d["bv_row"] = np.ascontiguousarray(
        np.asarray(inp["L0_attn_bv"], np.float32).reshape(1, D))
    d["g1_col"] = _col(inp["L0_attn_g1"], KC)
    d["be1_col"] = _col(inp["L0_attn_beta1"], KC)
    d["g2_col"] = _col(inp["L0_attn_g2"], KC)
    d["be2_col"] = _col(inp["L0_attn_beta2"], KC)
    d["rwT"] = np.ascontiguousarray(
        np.asarray(inp["L0_ffn_router_w"], np.float32).T.reshape(KC, 128, NE)
        .transpose(1, 0, 2).reshape(128, KC * NE))
    d["rb_row"] = np.ascontiguousarray(
        np.asarray(inp["L0_ffn_router_bias"], np.float32).reshape(1, NE))
    d["up_p"] = np.ascontiguousarray(inp["L0_ffn_up_proj"], np.float32)
    d["gate_p"] = np.ascontiguousarray(inp["L0_ffn_gate_proj"], np.float32)
    d["down_p"] = np.ascontiguousarray(inp["L0_ffn_down_proj"], np.float32)
    d["shgT"] = np.ascontiguousarray(np.asarray(inp["L0_ffn_sh_gate"], np.float32).T)
    d["shuT"] = np.ascontiguousarray(np.asarray(inp["L0_ffn_sh_up"], np.float32).T)
    d["shdT"] = np.ascontiguousarray(np.asarray(inp["L0_ffn_sh_down"], np.float32).T)
    d["onesf"] = np.ones((1, 128), np.float32)
    d["ones128b"] = np.ones((128, 1), BF)
    d["ones128f"] = np.ones((128, 1), np.float32)
    esel = np.zeros((H, D), np.float32)
    for h in range(H):
        esel[h, h * DH:(h + 1) * DH] = 1.0
    d["esel"] = esel.astype(BF)
    selb = np.zeros((NE, NE * 128), np.float32)
    for e in range(NE):
        selb[e, e * 128:(e + 1) * 128] = 1.0
    d["selbank"] = selb.astype(BF)
    d["id128"] = np.eye(128, dtype=np.float32).astype(BF)
    return d


_PROG_CACHE = {}


def _get_prog(t, ncores):
    key = (t, ncores)
    if key not in _PROG_CACHE:
        p = Prog(t=t, ncores=ncores)
        p.build()
        _PROG_CACHE[key] = p
    return _PROG_CACHE[key]


def kernel(**inputs):
    t = 2048
    ncores = 8
    p = _get_prog(t, ncores)
    shared = _prep_shared(inputs)
    x = np.asarray(inputs["L0_attn_x"], np.float32)[0]
    xT = np.ascontiguousarray(x.T)
    cvec = np.asarray(inputs["L0_c"], np.float32).reshape(D)
    wmod = np.ascontiguousarray(inputs["L0_wmod"], np.float32)
    rpc = D // ncores
    in_maps = []
    for c in range(ncores):
        m = dict(shared)
        m["xT"] = np.ascontiguousarray(xT[:, c * t:(c + 1) * t])
        m["c_col"] = _col(cvec[c * rpc:(c + 1) * rpc], rpc // 128)
        m["wmods"] = np.ascontiguousarray(wmod[c * rpc:(c + 1) * rpc, :])
        in_maps.append(m)
    res = run_bass_kernel_spmd(p.nc, in_maps, core_ids=list(range(ncores)))
    parts = [np.asarray(res.results[c]["y"]).T for c in range(ncores)]
    out = np.concatenate(parts, axis=0).reshape(1, ncores * t, D)
    return np.ascontiguousarray(out, dtype=np.float32)
